# version 05 (final)
# baseline (speedup 1.0000x reference)
import contextlib
from types import SimpleNamespace as NS
import numpy as np
import concourse.bass as bass
import concourse.mybir as mybir
from concourse.bass_utils import run_bass_kernel_spmd

F32 = mybir.dt.float32
BF16 = mybir.dt.bfloat16
I32 = mybir.dt.int32
AF = mybir.ActivationFunctionType
ALU = mybir.AluOpType
AX = mybir.AxisListType

D = 2048
NHC = 16
HD = 64
K = 16
FH = 1024
NRE = 16
NSH = 2
NE = NRE + NSH
EPS0 = 1e-5
EPST = 1e-6
MAGIC = 0x5F3759DF
ENGS = ("sync", "vector", "scalar", "tensor", "gpsimd")


class Prog:

    def __init__(self, nc, es):
        self.nc = nc
        self.q = {e: [] for e in ENGS}
        self.sems = {}
        for e in ENGS:
            self.sems[e] = [es.enter_context(nc.semaphore("s_" + e)), 0]
            self.sems["d" + e] = [es.enter_context(nc.semaphore("d_" + e)), 0]

    def emit(self, eng, fn, mark=True, dma=False):
        key = ("d" + eng) if dma else eng
        rec = self.sems[key]
        if mark:
            rec[1] += 16 if dma else 1

            def wrapped(e, fn=fn, rec=rec, k=16 if dma else 1):
                fn(e).then_inc(rec[0], k)
            self.q[eng].append(wrapped)
        else:
            self.q[eng].append(fn)
        return (key, rec[1])

    def wait(self, eng, token):
        key, val = token
        rec = self.sems[key]
        if val > 0:
            self.q[eng].append(lambda e, r=rec, v=val: e.wait_ge(r[0], v))

    def cur(self, eng):
        return (eng, self.sems[eng][1])

    def v(self, fn):
        t = self.emit("vector", fn)
        self.wait("vector", t)
        return t

    def run(self, block):
        for eng in ENGS:
            def make(eng):
                def body(e):
                    for fn in self.q[eng]:
                        fn(e)
                return body
            getattr(block, eng)(make(eng))


def emit_rsqrt(p, vcol, ycol, tcol, iters=3):
    yb = ycol.bitcast(I32)
    vb = vcol.bitcast(I32)
    p.v(lambda e: e.tensor_single_scalar(yb, vb, 1, ALU.logical_shift_right))
    p.v(lambda e: e.tensor_single_scalar(yb, yb, MAGIC, ALU.subtract))
    p.v(lambda e: e.tensor_single_scalar(yb, yb, -1, ALU.mult))
    for _ in range(iters):
        p.v(lambda e: e.tensor_mul(tcol, ycol, ycol))
        p.v(lambda e: e.scalar_tensor_tensor(tcol, tcol, -0.5, vcol, ALU.mult, ALU.mult))
        p.v(lambda e: e.scalar_tensor_tensor(ycol, tcol, 1.5, ycol, ALU.add, ALU.mult))
    return p.cur("vector")


def make_ctx(L, n_exp):
    c = NS()
    c.L = L
    c.n_exp = n_exp
    c.NM = L // K
    c.TOWN = L // 2
    c.NT = L // 128
    c.NTO = c.TOWN // 128
    c.NCHK = c.NM // 16
    c.NPC = L // 512
    c.nc = bass.Bass("TRN2", target_bir_lowering=False, debug=False)
    c.es = contextlib.ExitStack()
    c.p = Prog(c.nc, c.es)
    return c


def declare_io(c):
    nc, L, TOWN, n_exp = c.nc, c.L, c.TOWN, c.n_exp

    def din(name, shape, dtype=F32):
        return nc.dram_tensor(name, shape, dtype, kind="ExternalInput").ap()

    c.x_in = din("x_b", [L, D])
    c.x_own = din("x_own", [TOWN, D])
    c.qw = din("qw_t", [D, NHC * HD])
    c.kw = din("kw_t", [D, NHC * HD])
    c.vw = din("vw_t", [D, NHC * HD])
    c.ow = din("ow_t", [D, D])
    c.lrw = din("lrw_t", [D, NHC])
    c.lrb = din("lrb", [NHC, 1])
    c.w1_0 = din("w1_init", [128, 512])
    c.b1_0 = din("b1_init", [128, 128])
    c.gam_in = din("gam_t", [128, 128])
    c.bet_in = din("bet_t", [128, 128])
    c.lti_c = din("lti_col", [128, 1])
    c.lti15 = din("lti15_col", [128, 1])
    c.invi_c = din("invidx_col", [128, 1])
    c.mketa = din("masketa", [128, 64])
    c.onesbd = din("onesbd", [128, 32], BF16)
    c.bdmask_rep = din("bdmask_rep", [128, 2048], BF16)
    c.ident_bf = din("ident_bf", [128, 128], BF16)
    c.ident_f = din("ident_f", [128, 128])
    c.m0_in = din("m0col", [128, 1])
    c.m0mb = din("m0mask_b", [128, 1024], BF16)
    c.m0mf = din("m0mask_f", [128, 2048])
    c.postw_in = din("postw_row", [1, D])
    c.postb_in = din("postb_row", [1, D])
    c.onesrow_in = din("onesrow", [1, 128])
    c.wa_in = din("wa_t", [D, NRE])
    c.fg = din("f_g", [n_exp * 16 * 128, FH])
    c.fw1 = din("f_w1", [n_exp * 16 * 128, FH])
    c.fw2 = din("f_w2", [n_exp * 8 * 128, D])
    c.fgb = din("f_gb", [128, n_exp * 8])
    c.fb1 = din("f_b1", [128, n_exp * 8])
    c.frb2 = din("f_rb2", [128, D])
    c.fsb2a = din("f_sb2a", [1, D])
    c.fsb2b = din("f_sb2b", [1, D])

    c.out = nc.dram_tensor("out", [TOWN, D], F32, kind="ExternalOutput").ap()

    def dsc(name, shape, dtype=F32):
        return nc.dram_tensor(name, shape, dtype, kind="Internal").ap()

    c.xn_tmD = dsc("xn_tmD", [L, D])
    c.xnT_D = dsc("xnT_D", [D, L], BF16)
    c.kfmD = dsc("kfmD", [NHC * HD, L], BF16)
    c.qfmD = dsc("qfmD", [NHC * HD, L], BF16)
    c.qtmD = dsc("qtmD", [L, NHC * HD])
    c.ktmD = dsc("ktmD", [L, NHC * HD])
    c.vtmD = dsc("vtmD", [L, NHC * HD])
    c.siglrD = dsc("siglrD", [c.NM, 256])
    c.ysD = dsc("ysD", [L, NHC * HD])
    c.ccout = dsc("ccout", [2 * L, NHC * HD])
    c.wbG = dsc("wbG", [n_exp * 16 * 128, FH], BF16)
    c.wbW1 = dsc("wbW1", [n_exp * 16 * 128, FH], BF16)
    c.wbW2 = dsc("wbW2", [n_exp * 8 * 128, D], BF16)
    c.wbO = dsc("wbO", [D, D], BF16)

    c.pb = [c.es.enter_context(nc.psum_tensor(f"pb{i}", [128, 512], F32)) for i in range(8)]
    c.pbB = [t.ap().bitcast(BF16) for t in c.pb]

    c.idb = c.es.enter_context(nc.sbuf_tensor("idb", [128, 128], BF16))
    c.idf = c.es.enter_context(nc.sbuf_tensor("idf", [128, 128], F32))
    c.m0c = c.es.enter_context(nc.sbuf_tensor("m0c", [128, 1], F32))
    c.ones_sb = c.es.enter_context(nc.sbuf_tensor("ones_sb", [1, 128], F32))
    p = c.p
    p.emit("sync", lambda e: e.dma_start(out=c.idb[:], in_=c.ident_bf[:]), dma=True)
    p.emit("sync", lambda e: e.dma_start(out=c.idf[:], in_=c.ident_f[:]), dma=True)
    p.emit("sync", lambda e: e.dma_start(out=c.m0c[:], in_=c.m0_in[:]), dma=True)
    c.CONST_LD = p.emit("sync", lambda e: e.dma_start(out=c.ones_sb[:], in_=c.onesrow_in[:]), dma=True)


def emit_phase1(c):
    p, nc, L = c.p, c.nc, c.L
    WCOL = 3 * NHC * HD + 16
    sb1 = contextlib.ExitStack()
    wqkv = sb1.enter_context(nc.sbuf_tensor("wqkv", [128, 16 * WCOL], BF16))
    xtile = sb1.enter_context(nc.sbuf_tensor("xtile", [128, 2 * D], F32))
    xnf = sb1.enter_context(nc.sbuf_tensor("xnf", [128, 2 * D], F32))
    xnb = sb1.enter_context(nc.sbuf_tensor("xnb", [128, 2 * D], BF16))
    xnTc = sb1.enter_context(nc.sbuf_tensor("xnTc", [128, 16 * 512], BF16))
    lncol = sb1.enter_context(nc.sbuf_tensor("lncol", [128, 8], F32))
    trevac = sb1.enter_context(nc.sbuf_tensor("trevac", [128, 2 * 16 * 128], BF16))
    pevac = sb1.enter_context(nc.sbuf_tensor("pevac", [128, 4 * 512], BF16))
    tfb = sb1.enter_context(nc.sbuf_tensor("tfb", [128, 8 * 128], F32))
    sigev = sb1.enter_context(nc.sbuf_tensor("sigev", [16, 2 * 512], F32))
    lrbc = sb1.enter_context(nc.sbuf_tensor("lrbc", [16, 1], F32))
    stg = [sb1.enter_context(nc.sbuf_tensor(f"stg{i}", [128, 1024], F32)) for i in range(2)]
    wstg = [sb1.enter_context(nc.sbuf_tensor("wstg0", [128, D], F32))] * 2
    cbf = sb1.enter_context(nc.sbuf_tensor("cbf", [128, 2 * 2048], BF16))

    lrb_ld = p.emit("sync", lambda e: e.dma_start(out=lrbc[:], in_=c.lrb[:]), dma=True)

    wsrcs = [(c.qw, 0, NHC * HD), (c.kw, NHC * HD, NHC * HD), (c.vw, 2 * NHC * HD, NHC * HD),
             (c.lrw, 3 * NHC * HD, NHC)]
    wcasts = []
    i = 0
    for kk in range(16):
        for (src, coff, w) in wsrcs:
            dst = wqkv[:, kk * WCOL + coff: kk * WCOL + coff + w]
            sap = src[kk * 128:(kk + 1) * 128, :]
            if i >= 2:
                p.wait("sync", wcasts[i - 2])
            cld = p.emit("sync", lambda e, d=stg[i % 2], s=sap, w=w:
                         e.dma_start(out=d[:, 0:w], in_=s), dma=True)
            p.wait("scalar", cld)
            wcasts.append(p.emit("scalar", lambda e, d=dst, s=stg[i % 2], w=w: e.copy(d, s[:, 0:w])))
            i += 1
    c.WCAST_LAST = wcasts[-1]

    cast_jobs = []
    for e_i in range(c.n_exp):
        for kk in range(16):
            r = (e_i * 16 + kk) * 128
            cast_jobs.append((c.fg[r:r + 128, :], c.wbG[r:r + 128, :], FH))
            cast_jobs.append((c.fw1[r:r + 128, :], c.wbW1[r:r + 128, :], FH))
        for jj in range(8):
            r = (e_i * 8 + jj) * 128
            cast_jobs.append((c.fw2[r:r + 128, :], c.wbW2[r:r + 128, :], D))
    for kk in range(16):
        cast_jobs.append((c.ow[kk * 128:(kk + 1) * 128, :], c.wbO[kk * 128:(kk + 1) * 128, :], D))
    fw_store = []
    for i, (src, dst, w) in enumerate(cast_jobs):
        if i >= 1:
            p.wait("gpsimd", fw_store[i - 1])
        cld = p.emit("gpsimd", lambda e, d=wstg[i % 2], s=src, w=w:
                     e.dma_start(out=d[:, 0:w], in_=s), dma=True)
        p.wait("gpsimd", cld)
        ccp = p.emit("gpsimd", lambda e, d=cbf[:, (i % 2) * 2048:(i % 2) * 2048 + 2048],
                     s=wstg[i % 2], w=w: e.tensor_copy(d[:, 0:w], s[:, 0:w]))
        p.wait("gpsimd", ccp)
        fw_store.append(
            p.emit("gpsimd", lambda e, d=dst, s=cbf[:, (i % 2) * 2048:(i % 2) * 2048 + 2048], w=w:
                   e.dma_start(out=d, in_=s[:, 0:w]), dma=True))
    c.FW_LAST = fw_store[-1]

    sq_done = []
    xn_store = []
    xnt_store = []
    prev_ev = None
    for it in range(c.NT):
        par = it % 2
        xt = xtile[:, par * D:(par + 1) * D]
        xn_f = xnf[:, par * D:(par + 1) * D]
        xn_b = xnb[:, par * D:(par + 1) * D]
        if it >= 2:
            p.wait("sync", sq_done[it - 2])
            p.wait("sync", xnt_store[it - 2])
        xldc = p.emit("sync", lambda e, d=xt, s=c.x_in[it * 128:(it + 1) * 128, :]:
                      e.dma_start(out=d, in_=s), dma=True)
        p.wait("vector", xldc)
        p.v(lambda e, xt=xt: e.reduce_sum(lncol[:, 0:1], xt, axis=AX.X))
        muc = p.v(lambda e: e.tensor_single_scalar(lncol[:, 1:2], lncol[:, 0:1], -1.0 / D, ALU.mult))
        p.wait("scalar", muc)
        if it >= 2:
            p.wait("scalar", xn_store[it - 2])
        uc = p.emit("scalar", lambda e, d=xn_f, s=xt: e.add(d, s, lncol[:, 1:2]))
        p.wait("scalar", uc)
        sq_done.append(p.emit("scalar", lambda e, s=xn_f, d=xt:
                              e.activation(d, s, AF.Square, accum_out=lncol[:, 2:3])))
        p.wait("vector", sq_done[-1])
        p.v(lambda e: e.tensor_scalar(lncol[:, 3:4], lncol[:, 2:3], 1.0 / D, EPS0, ALU.mult, ALU.add))
        emit_rsqrt(p, lncol[:, 3:4], lncol[:, 4:5], lncol[:, 5:6])
        p.v(lambda e, d=xn_f: e.tensor_scalar(d, d, lncol[:, 4:5], None, ALU.mult))
        nbc = p.v(lambda e, d=xn_b, s=xn_f: e.tensor_copy(d, s))
        p.wait("gpsimd", nbc)
        xn_store.append(p.emit("gpsimd", lambda e, d=c.xn_tmD[it * 128:(it + 1) * 128, :], s=xn_f:
                               e.dma_start(out=d, in_=s), dma=True))
        p.wait("tensor", nbc)
        ev = []
        for bk in range(16):
            slot = bk % 2
            if bk >= 2:
                p.wait("tensor", ev[bk - 2])
            elif prev_ev is not None:
                p.wait("tensor", prev_ev[14 + bk])
            trc = p.emit("tensor", lambda e, d=c.pbB[slot][:, 0:128],
                         s=xn_b[:, bk * 128:(bk + 1) * 128]: e.transpose(d, s, c.idb[:]))
            p.wait("vector", trc)
            ev.append(p.emit("vector", lambda e,
                             d=trevac[:, (par * 16 + bk) * 128:(par * 16 + bk + 1) * 128],
                             s=c.pbB[slot][:, 0:128]: e.tensor_copy(d, s)))
        prev_ev = ev
        p.wait("gpsimd", ev[-1])
        xnt_store.append(
            p.emit("gpsimd", lambda e, it=it, par=par:
                   e.dma_start(out=c.xnT_D.rearrange("(k p) t -> p k t", p=128)[:, :, it * 128:(it + 1) * 128],
                               in_=trevac[:, par * 2048:(par + 1) * 2048].rearrange("p (k t) -> p k t", k=16)),
                   dma=True))
    c.XN_LAST = xn_store[-1]
    c.XNT_LAST = xnt_store[-1]

    OFC = 3 * NHC * HD // 128
    p.wait("tensor", c.WCAST_LAST)
    fm_store, tm_store, sg_store, pjev, tmev, lr_mm = [], [], [], [], [], []
    for ch in range(c.NPC):
        t0 = ch * 512
        p.wait("sync", c.XNT_LAST)
        if ch >= 1:
            p.wait("sync", lr_mm[-1])
        xtl = p.emit("sync", lambda e, t0=t0:
                     e.dma_start(out=xnTc[:].rearrange("p (k t) -> p k t", k=16),
                                 in_=c.xnT_D.rearrange("(k p) t -> p k t", p=128)[:, :, t0:t0 + 512]),
                     dma=True)
        p.wait("tensor", xtl)
        for oc in range(OFC):
            gidx = ch * OFC + oc
            slot = oc % 2
            if gidx >= 2:
                p.wait("tensor", pjev[gidx - 2])
            mmc = None
            for kk in range(16):
                mmc = p.emit("tensor", lambda e, kk=kk, oc=oc, slot=slot:
                             e.matmul(c.pb[slot][:, :],
                                      wqkv[:, kk * WCOL + oc * 128:kk * WCOL + (oc + 1) * 128],
                                      xnTc[:, kk * 512:(kk + 1) * 512],
                                      start=(kk == 0), stop=(kk == 15)),
                             mark=(kk == 15))
            p.wait("vector", mmc)
            if gidx >= 4:
                p.wait("vector", tr_last[gidx - 4])
                if fmst_of[gidx - 4] is not None:
                    p.wait("vector", fmst_of[gidx - 4])
            if gidx == 0:
                tr_last, fmst_of = {}, {}
            pjev.append(p.emit("vector", lambda e, d=pevac[:, (oc % 4) * 512:(oc % 4) * 512 + 512],
                               s=c.pb[slot][:, :]: e.tensor_copy(d, s)))
            src_t = ["q", "k", "v"][oc // 8]
            fr = (oc % 8) * 128
            if src_t in ("k", "q"):
                dstD = c.kfmD if src_t == "k" else c.qfmD
                p.wait("gpsimd", pjev[-1])
                fm_store.append(
                    p.emit("gpsimd", lambda e, d=dstD[fr:fr + 128, t0:t0 + 512],
                           s=pevac[:, (oc % 4) * 512:(oc % 4) * 512 + 512]:
                           e.dma_start(out=d, in_=s), dma=True))
                fmst_of[gidx] = fm_store[-1]
            else:
                fmst_of[gidx] = None
            dstT = {"q": c.qtmD, "k": c.ktmD, "v": c.vtmD}[src_t]
            p.wait("tensor", pjev[-1])
            for b4 in range(4):
                gt = len(tmev)
                tslot = 2 + (gt % 2)
                if gt >= 2:
                    p.wait("tensor", tmev[gt - 2])
                trc = p.emit("tensor", lambda e, d=c.pbB[tslot][:, 0:128],
                             s=pevac[:, (oc % 4) * 512 + b4 * 128:(oc % 4) * 512 + (b4 + 1) * 128]:
                             e.transpose(d, s, c.idb[:]))
                p.wait("vector", trc)
                if gt >= 8:
                    p.wait("vector", tm_store[gt - 8])
                tmev.append(p.emit("vector", lambda e, d=tfb[:, (gt % 8) * 128:(gt % 8) * 128 + 128],
                                   s=c.pbB[tslot][:, 0:128]: e.tensor_copy(d, s)))
                p.wait("gpsimd", tmev[-1])
                tm_store.append(
                    p.emit("gpsimd", lambda e, d=dstT[t0 + b4 * 128:t0 + (b4 + 1) * 128, fr:fr + 128],
                           s=tfb[:, (gt % 8) * 128:(gt % 8) * 128 + 128]:
                           e.dma_start(out=d, in_=s), dma=True))
            tr_last[gidx] = trc
        if ch >= 1:
            p.wait("tensor", sgc)
        mmc = None
        for kk in range(16):
            mmc = p.emit("tensor", lambda e, kk=kk:
                         e.matmul(c.pb[4][0:16, :],
                                  wqkv[:, kk * WCOL + 3 * NHC * HD:kk * WCOL + 3 * NHC * HD + 16],
                                  xnTc[:, kk * 512:(kk + 1) * 512],
                                  start=(kk == 0), stop=(kk == 15)),
                         mark=(kk == 15))
        lr_mm.append(mmc)
        p.wait("scalar", lrb_ld)
        p.wait("scalar", mmc)
        if ch >= 2:
            p.wait("scalar", sg_store[ch - 2])
        sgc = p.emit("scalar", lambda e, d=sigev[:, (ch % 2) * 512:(ch % 2) * 512 + 512],
                     s=c.pb[4][0:16, :]: e.activation(d, s, AF.Sigmoid, bias=lrbc[:, 0:1]))
        p.wait("gpsimd", sgc)
        sg_store.append(
            p.emit("gpsimd", lambda e, ch=ch:
                   e.dma_start(out=c.siglrD.rearrange("n (h s) -> h n s", s=16)[:, ch * 32:(ch + 1) * 32, :],
                               in_=sigev[:, (ch % 2) * 512:(ch % 2) * 512 + 512]
                               .rearrange("h (n s) -> h n s", s=16)),
                   dma=True))
    c.P1_TM_LAST = tm_store[-1]
    c.P1_FM_LAST = fm_store[-1]
    c.P1_SG_LAST = sg_store[-1]
    c.P1_PJ_LAST = lr_mm[-1]
    sb1.close()



def emit_phase2(c):
    p, nc, L, NM, NCHK = c.p, c.nc, c.L, c.NM, c.NCHK
    sb2 = contextlib.ExitStack()
    sbt = lambda n, sh, dt=F32: sb2.enter_context(nc.sbuf_tensor(n, sh, dt))
    fmk = sbt("fmk", [128, 2 * 8 * 256], BF16)
    fmq = sbt("fmq", [128, 2 * 8 * 256], BF16)
    tmq = sbt("tmq", [128, 2 * 2048])
    tmk = sbt("tmk", [128, 2 * 2048])
    tmv = sbt("tmv", [128, 2 * 2048])
    bdk = sbt("bdk", [128, 2 * 8 * 16 * 32], BF16)
    bdq = sbt("bdq", [128, 2 * 8 * 16 * 32], BF16)
    bdxk = sbt("bdxk", [128, 2 * 2 * 2048], BF16)
    bdmr = sbt("bdmr", [128, 2048], BF16)
    sigrow = sbt("sigrow", [1, 2 * 4096])
    yschunk = sbt("yschunk", [128, 2 * 2048])
    wst = sbt("wst", [128, 512])
    wstb = sbt("wstb", [128, 512], BF16)
    b1st = sbt("b1st", [128, 128])
    gam = sbt("gam", [128, 128])
    bet = sbt("bet", [128, 128])
    gam2 = sbt("gam2", [128, 128])
    gbt = sbt("gbt", [128, 128])
    mket = sbt("mket", [128, 64])
    ones128 = sbt("ones128", [128, 32], BF16)
    tokc = sbt("tokc", [128, 4])
    mtt = sbt("mtt", [128, 64], BF16)
    ue = sbt("ue", [128, 128])
    xh = sbt("xh", [128, 128])
    gxh = sbt("gxh", [128, 128])
    sc1 = sbt("sc1", [128, 128])
    sc2 = sbt("sc2", [128, 128])
    ctt = sbt("ctt", [128, 128])
    zbar = sbt("zbar", [128, 128])
    grad = sbt("gradp", [128, 128], BF16)
    cols = sbt("cols", [128, 16])

    p.wait("sync", c.P1_PJ_LAST)
    p.wait("sync", c.P1_TM_LAST)
    p.wait("sync", c.P1_FM_LAST)
    p.wait("sync", c.P1_SG_LAST)
    p.wait("sync", p.cur("vector"))
    p.wait("sync", p.cur("scalar"))
    ini = []
    for (dst, src) in [(wst[:], c.w1_0[:]), (b1st[:], c.b1_0[:]), (gam[:], c.gam_in[:]),
                       (bet[:], c.bet_in[:]), (mket[:], c.mketa[:]), (ones128[:], c.onesbd[:]),
                       (bdmr[:], c.bdmask_rep[:]), (tokc[:, 3:4], c.lti_c[:]),
                       (cols[:, 15:16], c.lti15[:]), (cols[:, 14:15], c.invi_c[:])]:
        ini.append(p.emit("sync", lambda e, d=dst, s=src: e.dma_start(out=d, in_=s), dma=True))
    p.wait("vector", c.P1_PJ_LAST)
    p.wait("vector", ini[-1])
    p.v(lambda e: e.memset(bdk[:], 0.0))
    p.v(lambda e: e.memset(bdq[:], 0.0))
    p.v(lambda e: e.tensor_scalar(tokc[:, 0:1], cols[:, 14:15], tokc[:, 3:4], 0.0, ALU.add, ALU.max))
    p.v(lambda e: e.tensor_single_scalar(tokc[:, 1:2], tokc[:, 0:1], -1.0, ALU.mult))
    p.v(lambda e: e.tensor_scalar(tokc[:, 2:3], cols[:, 15:16], 1.0 / 16, 0.0, ALU.add, ALU.max))
    p.v(lambda e: e.tensor_single_scalar(tokc[:, 2:3], tokc[:, 2:3], -1.0, ALU.mult))
    p.v(lambda e: e.tensor_mul(gam2[:], gam[:], gam[:]))
    p.v(lambda e: e.tensor_mul(gbt[:], gam[:], bet[:]))
    st_tok = p.v(lambda e: e.tensor_copy(wstb[:], wst[:]))
    act_prev = None
    vec_z1_prev = None

    p.wait("sync", c.P1_TM_LAST)
    p.wait("sync", c.P1_FM_LAST)
    p.wait("sync", c.P1_SG_LAST)

    Z1 = c.pb[0][:, 0:128]
    XQW = c.pb[0][:, 128:256]
    GRAM = c.pb[0][:, 256:320]
    SIG = c.pb[0][:, 320:322]
    Z1BT = c.pb[1][:, 0:128]
    B1D = c.pb[1][:, 128:256]
    W1D = c.pb[2][:, 0:512]

    ys_store = []
    chunk_last_mm = []
    chunk_last_v = []
    chunk_last_g = []
    ys_prev = None
    for chk in range(NCHK):
        t0 = chk * 256
        buf = chk % 2
        if chk >= 2:
            p.wait("sync", chunk_last_mm[chk - 2])
            p.wait("sync", chunk_last_v[chk - 2])
            p.wait("sync", chunk_last_g[chk - 2])
            p.wait("vector", ys_store[(chk - 2) * 16 + 15])
        lds = []
        for (gp, dstD) in [(fmk, c.kfmD), (fmq, c.qfmD)]:
            for pr8 in range(8):
                lds.append(p.emit("sync", lambda e, d=gp[:, (buf * 8 + pr8) * 256:(buf * 8 + pr8 + 1) * 256],
                                  s=dstD[pr8 * 128:(pr8 + 1) * 128, t0:t0 + 256]:
                                  e.dma_start(out=d, in_=s), dma=True))
        for (tmb, dD) in [(tmq, c.qtmD), (tmk, c.ktmD), (tmv, c.vtmD)]:
            for g in range(2):
                for cc in range(8):
                    src = dD.rearrange("(nn s) (hh f) -> nn s hh f", s=16, f=HD)
                    src = src[16 * chk:16 * chk + 16, :, 8 * g + cc, :].transpose([1, 0, 2])
                    dst = tmb[16 * cc:16 * cc + 16,
                              buf * 2048 + g * 1024: buf * 2048 + (g + 1) * 1024]
                    lds.append(p.emit("sync", lambda e, d=dst, s=src:
                                      e.dma_start(out=d, in_=s.opt()), dma=True))
        lds.append(p.emit("sync", lambda e, d=sigrow[0:1, buf * 4096:(buf + 1) * 4096],
                          s=c.siglrD[chk * 16:(chk + 1) * 16, :]:
                          e.dma_start(out=d, in_=s.rearrange("n cc -> (n cc)").unsqueeze(0)), dma=True))
        LDC = lds[-1]
        p.wait("vector", LDC)
        stv = None
        for (bd, fm) in [(bdk, fmk), (bdq, fmq)]:
            for pr8 in range(8):
                srcs = fm[:, (buf * 8 + pr8) * 256:(buf * 8 + pr8 + 1) * 256].rearrange(
                    "p (n ss) -> p n ss", ss=16)
                dsts = bd[:, (buf * 8 + pr8) * 512:(buf * 8 + pr8 + 1) * 512].rearrange(
                    "p (n w) -> p n w", w=32)
                stv = p.emit("vector", lambda e, d=dsts, s=srcs:
                             e.tensor_copy(d[0:64, :, 0:16], s[0:64, :, :]))
                stv = p.emit("vector", lambda e, d=dsts, s=srcs:
                             e.tensor_copy(d[64:128, :, 16:32], s[64:128, :, :]))
        BDKQ = stv
        p.wait("gpsimd", LDC)
        gx = None
        for g in range(2):
            dst = bdxk[:, (buf * 2 + g) * 2048:(buf * 2 + g + 1) * 2048].rearrange(
                "p (n w) -> p n w", w=128)
            src = tmk[:, buf * 2048 + g * 1024: buf * 2048 + (g + 1) * 1024].rearrange(
                "p (n f) -> p n f", f=64)
            gx = p.emit("gpsimd", lambda e, d=dst, s=src: e.tensor_copy(d[:, :, 0:64], s))
            p.wait("gpsimd", gx)
            gx = p.emit("gpsimd", lambda e, d=dst, s=src: e.tensor_copy(d[:, :, 64:128], s))
            p.wait("gpsimd", gx)
            gx = p.emit("gpsimd", lambda e, d=bdxk[:, (buf * 2 + g) * 2048:(buf * 2 + g + 1) * 2048]:
                        e.tensor_tensor(d, d, bdmr[:], ALU.mult))
            p.wait("gpsimd", gx)
        BDXK = gx

        for n in range(16):
            sl128 = slice(buf * 2048 + n * 128, buf * 2048 + (n + 1) * 128)
            p.wait("tensor", st_tok)
            p.wait("tensor", BDKQ)
            if act_prev is not None:
                p.wait("tensor", act_prev)
            if vec_z1_prev is not None:
                p.wait("tensor", vec_z1_prev)
            mt = None
            for g in range(2):
                for pr in range(4):
                    gp8 = g * 4 + pr
                    lz = bdk[:, (buf * 8 + gp8) * 512 + n * 32:(buf * 8 + gp8) * 512 + (n + 1) * 32]
                    lq = bdq[:, (buf * 8 + gp8) * 512 + n * 32:(buf * 8 + gp8) * 512 + (n + 1) * 32]
                    rw = wstb[:, gp8 * 64:(gp8 + 1) * 64]
                    tp = (0, 32 * pr) if pr == 3 else None
                    mt = p.emit("tensor", lambda e, lz=lz, rw=rw, g=g, pr=pr, tp=tp:
                                e.matmul(Z1[32 * pr:32 * pr + 32, 64 * g:64 * g + 64], lz, rw,
                                         start=True, stop=True, tile_position=tp), mark=False)
                    mt = p.emit("tensor", lambda e, lq=lq, rw=rw, g=g, pr=pr, tp=tp:
                                e.matmul(XQW[32 * pr:32 * pr + 32, 64 * g:64 * g + 64], lq, rw,
                                         start=True, stop=True, tile_position=tp), mark=False)
                    mt = p.emit("tensor", lambda e, lz=lz, lq=lq, g=g, pr=pr, tp=tp:
                                e.matmul(GRAM[32 * pr:32 * pr + 32, 32 * g:32 * g + 32], lz, lq,
                                         start=True, stop=True, tile_position=tp), mark=False)
            for g in range(2):
                sr = sigrow[0:1, buf * 4096 + n * 256 + g * 128: buf * 4096 + n * 256 + (g + 1) * 128]
                mt = p.emit("tensor", lambda e, sr=sr, g=g:
                            e.matmul(SIG[:, g:g + 1], sr, c.ones_sb[0:1, 0:1],
                                     start=True, stop=True), mark=(g == 1))
            m1tok = mt
            p.wait("vector", m1tok)
            p.v(lambda e: e.tensor_add(sc2[:], Z1, b1st[:]))
            sc2v = sc2[:].rearrange("p (h f) -> p h f", h=2)
            p.v(lambda e: e.reduce_sum(cols[:, 0:2], sc2v, axis=AX.X))
            p.v(lambda e: e.tensor_single_scalar(cols[:, 2:4], cols[:, 0:2], -1.0 / HD, ALU.mult))
            p.wait("scalar", p.cur("vector"))
            a1 = p.emit("scalar", lambda e: e.add(ue[:, 0:64], sc2[:, 0:64], cols[:, 2:3]))
            a1 = p.emit("scalar", lambda e: e.add(ue[:, 64:128], sc2[:, 64:128], cols[:, 3:4]))
            p.wait("scalar", a1)
            a2 = p.emit("scalar", lambda e: e.activation(gxh[:, 0:64], ue[:, 0:64], AF.Square,
                                                         accum_out=cols[:, 4:5]))
            a2 = p.emit("scalar", lambda e: e.activation(gxh[:, 64:128], ue[:, 64:128], AF.Square,
                                                         accum_out=cols[:, 5:6]))
            act_tok = a2
            p.wait("gpsimd", BDXK)
            if ys_prev is not None:
                p.wait("gpsimd", ys_prev)
            tv0 = tmv[:, buf * 2048 + n * 64: buf * 2048 + (n + 1) * 64]
            tv1 = tmv[:, buf * 2048 + 1024 + n * 64: buf * 2048 + 1024 + (n + 1) * 64]
            tk0 = tmk[:, buf * 2048 + n * 64: buf * 2048 + (n + 1) * 64]
            tk1 = tmk[:, buf * 2048 + 1024 + n * 64: buf * 2048 + 1024 + (n + 1) * 64]
            g1 = p.emit("gpsimd", lambda e, a=tv0, b=tk0:
                        e.tensor_tensor(sc1[:, 0:64], a, b, ALU.subtract))
            g1 = p.emit("gpsimd", lambda e, a=tv1, b=tk1:
                        e.tensor_tensor(sc1[:, 64:128], a, b, ALU.subtract))
            p.wait("gpsimd", g1)
            g1 = p.emit("gpsimd", lambda e: e.tensor_tensor(sc1[:], sc1[:], gam[:], ALU.mult))
            p.wait("gpsimd", g1)
            g1 = p.emit("gpsimd", lambda e: e.tensor_tensor(ctt[:], gbt[:], sc1[:], ALU.subtract))
            ct_tok = g1
            p.wait("vector", act_tok)
            p.v(lambda e: e.tensor_scalar(cols[:, 6:8], cols[:, 4:6], 1.0 / HD, EPST,
                                          ALU.mult, ALU.add))
            emit_rsqrt(p, cols[:, 6:8], cols[:, 8:10], cols[:, 10:12], iters=3)
            p.v(lambda e: e.tensor_scalar(xh[:, 0:64], ue[:, 0:64], cols[:, 8:9], None, ALU.mult))
            xh_tok = p.v(lambda e: e.tensor_scalar(xh[:, 64:128], ue[:, 64:128], cols[:, 9:10],
                                                   None, ALU.mult))
            p.wait("gpsimd", xh_tok)
            g2 = p.emit("gpsimd", lambda e: e.tensor_tensor(gxh[:], gam2[:], xh[:], ALU.mult))
            p.wait("gpsimd", g2)
            p.wait("gpsimd", ct_tok)
            g2 = p.emit("gpsimd", lambda e: e.tensor_tensor(gxh[:], gxh[:], ctt[:], ALU.add))
            gxh_tok = g2
            p.wait("vector", gxh_tok)
            gxhv = gxh[:].rearrange("p (h f) -> p h f", h=2)
            p.v(lambda e: e.reduce_sum(cols[:, 0:2], gxhv, axis=AX.X))
            p.v(lambda e: e.tensor_mul(sc2[:], gxh[:], xh[:]))
            sc2v = sc2[:].rearrange("p (h f) -> p h f", h=2)
            p.v(lambda e: e.reduce_sum(cols[:, 2:4], sc2v, axis=AX.X))
            p.v(lambda e: e.tensor_single_scalar(cols[:, 0:4], cols[:, 0:4], 1.0 / HD, ALU.mult))
            p.v(lambda e: e.tensor_scalar(sc1[:, 0:64], xh[:, 0:64], cols[:, 2:3], cols[:, 0:1],
                                          ALU.mult, ALU.add))
            p.v(lambda e: e.tensor_scalar(sc1[:, 64:128], xh[:, 64:128], cols[:, 3:4], cols[:, 1:2],
                                          ALU.mult, ALU.add))
            p.v(lambda e: e.tensor_sub(sc1[:], gxh[:], sc1[:]))
            p.v(lambda e: e.scalar_tensor_tensor(cols[:, 4:6], cols[:, 8:10], 1.0 / HD, SIG,
                                                 ALU.mult, ALU.mult))
            p.v(lambda e: e.tensor_scalar(grad[:, 0:64], sc1[:, 0:64], cols[:, 4:5], None, ALU.mult))
            gr_tok = p.v(lambda e: e.tensor_scalar(grad[:, 64:128], sc1[:, 64:128], cols[:, 5:6],
                                                   None, ALU.mult))
            mt_tok = p.v(lambda e: e.scalar_tensor_tensor(mtt[:], GRAM, 1.0, mket[:],
                                                          ALU.add, ALU.mult))
            vec_z1_prev = mt_tok
            p.wait("tensor", gr_tok)
            p.wait("tensor", mt_tok)
            m2 = None
            for g in range(2):
                for pr in range(4):
                    gp8 = g * 4 + pr
                    tp = (32 * pr, 32 * pr)
                    gsl = grad[32 * pr:32 * pr + 32, 64 * g:64 * g + 64]
                    m2 = p.emit("tensor", lambda e, g=g, pr=pr, gsl=gsl, tp=tp:
                                e.matmul(Z1BT[32 * pr:32 * pr + 32, 64 * g:64 * g + 64],
                                         mtt[32 * pr:32 * pr + 32, 32 * g:32 * g + 32], gsl,
                                         start=True, stop=True, tile_position=tp), mark=False)
                    m2 = p.emit("tensor", lambda e, g=g, pr=pr, gsl=gsl, tp=tp:
                                e.matmul(B1D[32 * pr:32 * pr + 32, 64 * g:64 * g + 64],
                                         ones128[32 * pr:32 * pr + 32, :], gsl,
                                         start=True, stop=True, tile_position=tp), mark=False)
                    bxsl = bdxk[32 * pr:32 * pr + 32,
                                (buf * 2 + g) * 2048 + n * 128:(buf * 2 + g) * 2048 + (n + 1) * 128]
                    m2 = p.emit("tensor", lambda e, g=g, pr=pr, gsl=gsl, bxsl=bxsl, tp=(32 * pr, 0):
                                e.matmul(W1D[:, gp8 * 64:(gp8 + 1) * 64] if False else
                                         c.pb[2][:, (g * 4 + pr) * 64:(g * 4 + pr + 1) * 64],
                                         bxsl, gsl, start=True, stop=True, tile_position=tp),
                                mark=(g == 1 and pr == 3))
            m2tok = m2
            p.wait("vector", m2tok)
            p.v(lambda e: e.tensor_add(sc2[:], XQW, b1st[:]))
            p.v(lambda e: e.scalar_tensor_tensor(zbar[:], Z1BT, tokc[:, 1:2], sc2[:],
                                                 ALU.mult, ALU.add))
            p.v(lambda e: e.scalar_tensor_tensor(wst[:, 0:256], c.pb[2][:, 0:256], tokc[:, 2:3],
                                                 wst[:, 0:256], ALU.mult, ALU.add))
            p.v(lambda e: e.scalar_tensor_tensor(wst[:, 256:512], c.pb[2][:, 256:512], tokc[:, 2:3],
                                                 wst[:, 256:512], ALU.mult, ALU.add))
            st_tok = p.v(lambda e: e.tensor_copy(wstb[:], wst[:]))
            p.v(lambda e: e.scalar_tensor_tensor(b1st[:], B1D, tokc[:, 2:3], b1st[:],
                                                 ALU.mult, ALU.add))
            zv = zbar[:].rearrange("p (h f) -> p h f", h=2)
            p.v(lambda e: e.reduce_sum(cols[:, 0:2], zv, axis=AX.X))
            p.v(lambda e: e.tensor_single_scalar(cols[:, 2:4], cols[:, 0:2], -1.0 / HD, ALU.mult))
            p.wait("scalar", p.cur("vector"))
            a3 = p.emit("scalar", lambda e: e.add(ue[:, 0:64], zbar[:, 0:64], cols[:, 2:3]))
            a3 = p.emit("scalar", lambda e: e.add(ue[:, 64:128], zbar[:, 64:128], cols[:, 3:4]))
            p.wait("scalar", a3)
            a4 = p.emit("scalar", lambda e: e.activation(sc2[:, 0:64], ue[:, 0:64], AF.Square,
                                                         accum_out=cols[:, 4:5]))
            a4 = p.emit("scalar", lambda e: e.activation(sc2[:, 64:128], ue[:, 64:128], AF.Square,
                                                         accum_out=cols[:, 5:6]))
            act_prev = a4
            p.wait("vector", a4)
            p.v(lambda e: e.tensor_scalar(cols[:, 6:8], cols[:, 4:6], 1.0 / HD, EPST,
                                          ALU.mult, ALU.add))
            emit_rsqrt(p, cols[:, 6:8], cols[:, 8:10], cols[:, 10:12], iters=3)
            p.v(lambda e: e.tensor_scalar(sc1[:, 0:64], ue[:, 0:64], cols[:, 8:9], None, ALU.mult))
            p.v(lambda e: e.tensor_scalar(sc1[:, 64:128], ue[:, 64:128], cols[:, 9:10], None, ALU.mult))
            p.v(lambda e: e.tensor_mul(sc1[:], sc1[:], gam[:]))
            p.v(lambda e: e.tensor_add(sc1[:], sc1[:], bet[:]))
            tq0 = tmq[:, buf * 2048 + n * 64: buf * 2048 + (n + 1) * 64]
            tq1 = tmq[:, buf * 2048 + 1024 + n * 64: buf * 2048 + 1024 + (n + 1) * 64]
            yd0 = yschunk[:, buf * 2048 + n * 64: buf * 2048 + (n + 1) * 64]
            yd1 = yschunk[:, buf * 2048 + 1024 + n * 64: buf * 2048 + 1024 + (n + 1) * 64]
            p.v(lambda e, a=tq0, d=yd0: e.tensor_add(d, sc1[:, 0:64], a))
            ys_tok = p.v(lambda e, a=tq1, d=yd1: e.tensor_add(d, sc1[:, 64:128], a))
            ys_prev = ys_tok
        chunk_last_mm.append(m2tok)
        chunk_last_v.append(ys_tok)
        chunk_last_g.append(gxh_tok)
        p.wait("gpsimd", ys_tok)
        for g in range(2):
            for cc in range(8):
                dstys = c.ysD.rearrange("(nn s) (hh f) -> nn s hh f", s=16, f=HD)
                dstys = dstys[16 * chk:16 * chk + 16, :, 8 * g + cc, :].transpose([1, 0, 2])
                srcys = yschunk[16 * cc:16 * cc + 16,
                                buf * 2048 + g * 1024: buf * 2048 + (g + 1) * 1024]
                ys_store.append(p.emit("gpsimd", lambda e, d=dstys, s=srcys:
                                       e.dma_start(out=d, in_=s.opt()), dma=True))
    c.YS_LAST = ys_store[-1]
    sb2.close()


def emit_collective(c):
    p = c.p
    p.wait("gpsimd", c.YS_LAST)
    c.CC_TOK = p.emit("gpsimd", lambda e: e.collective_compute(
        "AllGather", ALU.bypass, [[2 * i, 2 * i + 1] for i in range(4)],
        [c.ysD.opt()], [c.ccout.opt()]))


def emit_stub_out(c):
    p, nc = c.p, c.nc
    sb = contextlib.ExitStack()
    zz = sb.enter_context(nc.sbuf_tensor("zz", [128, D], F32))
    zc = p.v(lambda e: e.memset(zz[:], 0.0))
    p.wait("gpsimd", zc)
    last = None
    for tt in range(c.NTO):
        last = p.emit("gpsimd", lambda e, d=c.out[tt * 128:(tt + 1) * 128, :]:
                      e.dma_start(out=d, in_=zz[:]), dma=True)
    emit_final_waits(c)
    sb.close()


def emit_final_waits(c):
    p = c.p
    for key in ("dsync", "dvector", "dscalar", "dtensor", "dgpsimd",
                "sync", "vector", "scalar", "tensor"):
        p.wait("gpsimd", (key, p.sems[key][1]))
    p.wait("gpsimd", ("dgpsimd", p.sems["dgpsimd"][1]))


def build_nc(L=4096, n_exp=NE, phases=1, use_collective=True, dbg_ffn_only=False, dbg_skip=(),
             dbg_ys=False):
    c = make_ctx(L, n_exp)
    declare_io(c)
    emit_phase1(c)
    if phases >= 2:
        emit_phase2(c)
        if use_collective:
            emit_collective(c)
    if dbg_ys:
        p = c.p
        p.wait("gpsimd", c.YS_LAST)
        if dbg_ys in ("cc0", "cc1") and use_collective:
            p.wait("gpsimd", c.CC_TOK)
            roff = 0 if dbg_ys == "cc0" else L
            src = c.ccout[roff:roff + L, :].rearrange("(a b) f -> a (b f)", b=2)
        else:
            src = c.ysD.rearrange("(a b) f -> a (b f)", b=2)
        last = p.emit("gpsimd", lambda e, src=src: e.dma_start(
            out=c.out[:, :], in_=src), dma=True)
        p.wait("sync", last)
        emit_final_waits(c)
        with c.es, c.nc.Block() as block:
            c.p.run(block)
        return c.nc
    if phases >= 3:
        emit_phase3(c, use_collective, dbg_ffn_only, dbg_skip)
    else:
        emit_stub_out(c)
    with c.es, c.nc.Block() as block:
        c.p.run(block)
    return c.nc



def emit_phase3(c, use_collective=True, dbg_ffn_only=False, dbg_skip=()):
    p, nc, L, TOWN, n_exp = c.p, c.nc, c.L, c.TOWN, c.n_exp
    TT = TOWN // 2
    NTB = TT // 128
    sb3 = contextlib.ExitStack()
    sbt = lambda n, sh, dt=F32: sb3.enter_context(nc.sbuf_tensor(n, sh, dt))
    xnTh = sbt("xnTh", [128, max(16 * TT, 16384)], BF16)
    xstg = sbt("xstg", [128, 2 * TT], BF16)
    wab = sbt("wab", [128, 16 * 16], BF16)
    wstg3 = sbt("wstg3", [128, 256])
    comb = sbt("comb", [128, NTB * 16])
    combT = sbt("combT", [128, NTB * 128])
    gbuf = sbt("gbuf", [128, 2 * 16 * 256], BF16)
    w1buf = sbt("w1buf", [128, 2 * 16 * 256], BF16)
    w2buf = sbt("w2buf", [128, 2 * 8 * 512], BF16)
    Yb = sbt("Yb", [128, max(8 * TT, 8192)], BF16)
    Gb = sbt("Gb", [128, 2 * 512])
    acc = sbt("acc", [128, NTB * 2048])
    gbc = sbt("gbc", [128, (NE) * 8])
    b1c3 = sbt("b1c3", [128, (NE) * 8])
    rb2s = sbt("rb2s", [128, D])
    sb2b = sbt("sb2b", [128, D])
    colz = sbt("colz", [128, 16])
    xnThF = xnTh.ap().bitcast(F32)
    pwrep = xnThF[:, 4096:6144]
    pbrep = xnThF[:, 6144:8192]
    w1f32 = w1buf.ap().bitcast(F32)
    ysf = w1f32[:, 0:2048]
    scr3 = w1f32[:, 2048:4096]
    ysn = xnTh[:, 0:4096]
    ysnT = xnTh[:, 4096:8192]
    xrs = Yb.ap().bitcast(F32)
    owbuf0 = w2buf[:, 0:8192]
    owbuf1 = gbuf[:, 0:8192]

    p.wait("sync", c.YS_LAST)
    p.wait("sync", p.cur("vector"))
    p.wait("sync", p.cur("scalar"))
    p.wait("sync", p.cur("tensor"))
    p.wait("sync", p.cur("gpsimd"))

    ini = []
    ini.append(p.emit("sync", lambda e: e.dma_start(out=gbc[:, 0:n_exp * 8], in_=c.fgb[:]), dma=True))
    ini.append(p.emit("sync", lambda e: e.dma_start(out=b1c3[:, 0:n_exp * 8], in_=c.fb1[:]), dma=True))
    ini.append(p.emit("sync", lambda e: e.dma_start(
        out=wstg3[:].rearrange("pp (k ee) -> pp k ee", ee=16), in_=c.wa_in.rearrange("(k pp) ee -> k pp ee", pp=128).transpose([1, 0, 2])), dma=True))
    ini.append(p.emit("sync", lambda e: e.dma_start(out=rb2s[:], in_=c.frb2[:]), dma=True))
    p.wait("vector", ini[-1])
    p.v(lambda e: e.tensor_copy(wab[:], wstg3[:]))

    MV = min(512, TOWN // 2)
    out_stores = []
    evlast = {}
    ge_toks, ge_toks2, y_toks, ev_toks, w2mm_toks, wld = {}, {}, {}, {}, {}, {}
    for tmove in range(2):
        tbase = tmove * TT
        if tmove >= 1:
            p.wait("sync", p.cur("tensor"))
            p.wait("sync", p.cur("vector"))
            p.wait("sync", p.cur("scalar"))
            if out_stores:
                p.wait("sync", out_stores[-1])
        xn_tok = None
        for k in range(16):
            lc0 = p.emit("sync", lambda e, k=k, d=xstg[:, 0:TT],
                         s=c.xnT_D[k * 128:(k + 1) * 128, tbase:tbase + TT]:
                         e.dma_start(out=d, in_=s), dma=True)
            lc1 = p.emit("sync", lambda e, k=k, d=xstg[:, TT:2 * TT],
                         s=c.xnT_D[k * 128:(k + 1) * 128, TOWN + tbase:TOWN + tbase + TT]:
                         e.dma_start(out=d, in_=s), dma=True)
            p.wait("vector", lc1)
            p.v(lambda e: e.tensor_sub(xstg[:, 0:TT], xstg[:, 0:TT], xstg[:, TT:2 * TT]))
            xn_tok = p.v(lambda e, d=xnTh[:, k * TT:(k + 1) * TT]:
                         e.scalar_tensor_tensor(d, xstg[:, 0:TT], c.m0c[:, 0:1],
                                                xstg[:, TT:2 * TT], ALU.mult, ALU.add))
            p.wait("sync", xn_tok)
        p.wait("tensor", xn_tok)
        sg_toks = []
        for tb in range(NTB):
            if sg_toks:
                p.wait("tensor", sg_toks[-1])
            mmt = None
            for k in range(16):
                mmt = p.emit("tensor", lambda e, k=k, tb=tb:
                             e.matmul(c.pb[7][0:128, 0:16], xnTh[:, k * TT + tb * 128:k * TT + (tb + 1) * 128],
                                      wab[:, k * 16:(k + 1) * 16], start=(k == 0), stop=(k == 15)),
                             mark=(k == 15))
            p.wait("scalar", mmt)
            if tb >= 1:
                p.wait("scalar", v_comb)
            sg = p.emit("scalar", lambda e, tb=tb: e.activation(wstg3[:, 0:16], c.pb[7][0:128, 0:16],
                                                                AF.Sigmoid))
            sg_toks.append(sg)
            p.wait("vector", sg)
            p.v(lambda e: e.reduce_max(colz[:, 0:1], wstg3[:, 0:16], axis=AX.X))
            p.v(lambda e: e.tensor_scalar(wstg3[:, 16:32], wstg3[:, 0:16], colz[:, 0:1], None, ALU.is_equal))
            p.v(lambda e: e.scalar_tensor_tensor(wstg3[:, 16:32], wstg3[:, 16:32], -1e30, wstg3[:, 0:16],
                                                 ALU.mult, ALU.add))
            p.v(lambda e: e.reduce_max(colz[:, 1:2], wstg3[:, 16:32], axis=AX.X))
            p.v(lambda e: e.tensor_scalar(colz[:, 2:3], colz[:, 0:1], colz[:, 1:2], None, ALU.add))
            p.v(lambda e: e.reciprocal(colz[:, 3:4], colz[:, 2:3]))
            p.v(lambda e: e.tensor_scalar(wstg3[:, 16:32], wstg3[:, 0:16], colz[:, 1:2], None, ALU.is_ge))
            p.v(lambda e: e.tensor_mul(wstg3[:, 16:32], wstg3[:, 16:32], wstg3[:, 0:16]))
            v_comb = p.v(lambda e, tb=tb: e.tensor_scalar(comb[:, tb * 16:(tb + 1) * 16],
                                                          wstg3[:, 16:32], colz[:, 3:4], None, ALU.mult))
            if "rcomb" in dbg_skip:
                v_comb = p.v(lambda e, tb=tb: e.memset(comb[:, tb * 16:(tb + 1) * 16], 0.0625))
            if "combT" not in dbg_skip:
                if tb == 0:
                    pz = p.v(lambda e: e.memset(acc[:, 0:128], 0.0))
                cpad = p.v(lambda e, tb=tb: e.tensor_copy(acc[:, 0:16],
                                                          comb[:, tb * 16:(tb + 1) * 16]))
                p.wait("tensor", cpad)
                if tb >= 1 or tmove >= 1:
                    p.wait("tensor", ctv)
                trt = p.emit("tensor", lambda e: e.transpose(
                    c.pb[6][:, 0:128], acc[:, 0:128], c.idf[:]))
                p.wait("vector", trt)
                ctv = p.v(lambda e, tb=tb: e.tensor_copy(combT[:, tb * 128:(tb + 1) * 128],
                                                         c.pb[6][:, 0:128]))
        if "snapcomb" in dbg_skip:
            p.v(lambda e: e.tensor_copy(wstg3[:, 64:64 + NTB * 16], comb[:, :]))
        ROUTED = p.cur("vector")

        def emit_wload(ei, jp):
            par = jp % 2
            for (buf_, wb_) in [(gbuf, c.wbG), (w1buf, c.wbW1)]:
                src = wb_.rearrange("(ek pp) f -> pp ek f", pp=128)[:, ei * 16:ei * 16 + 16,
                                                                   jp * 256:(jp + 1) * 256]
                key = (tmove, ei, jp, id(buf_))
                prev = None
                if jp >= 2:
                    prev = ge_toks.get((tmove, ei, jp - 2))
                elif (tmove, ei - 1, jp + 2) in ge_toks:
                    prev = ge_toks.get((tmove, ei - 1, jp + 2))
                elif tmove == 1 and ei == 0:
                    prev = ge_toks.get((0, n_exp - 1, jp + 2))
                if prev is not None:
                    p.wait("sync", prev)
                wld[key] = p.emit(
                    "sync", lambda e, d=buf_[:, par * 4096:(par + 1) * 4096], s=src:
                    e.dma_start(out=d.rearrange("p (k f) -> p k f", k=16), in_=s.opt()), dma=True)

        for ei in range(n_exp):
            if ei == 0:
                emit_wload(ei, 0)
                emit_wload(ei, 1)
            for j in range(8):
                jp = j // 2
                par = jp % 2
                jj = j % 2
                gp_ = c.pb[j % 2]
                hp_ = c.pb[2 + j % 2]
                p.wait("tensor", wld[(tmove, ei, jp, id(w1buf))])
                p.wait("tensor", wld[(tmove, ei, jp, id(gbuf))])
                if ei == 0 and j == 0:
                    p.wait("tensor", ROUTED)
                for tmv2 in range(TT // MV):
                    MVL = TT // MV - 1
                    if tmv2 >= 1:
                        kprev = (tmove, ei, j, tmv2 - 1)
                    elif j >= 2:
                        kprev = (tmove, ei, j - 2, MVL)
                    elif ei >= 1:
                        kprev = (tmove, ei - 1, 6 + j, MVL)
                    elif tmove == 1:
                        kprev = (0, n_exp - 1, 6 + j, MVL)
                    else:
                        kprev = None
                    if kprev is not None and kprev in y_toks:
                        p.wait("tensor", y_toks[kprev])
                        p.wait("tensor", ge_toks2[kprev])
                    mmg = mmh = None
                    for k in range(16):
                        lh = gbuf[:, par * 4096 + k * 256 + jj * 128: par * 4096 + k * 256 + (jj + 1) * 128]
                        rh = xnTh[:, k * TT + tmv2 * MV: k * TT + (tmv2 + 1) * MV]
                        mmg = p.emit("tensor", lambda e, k=k, lh=lh, rh=rh, d=gp_[:, 0:MV]:
                                     e.matmul(d, lh, rh, start=(k == 0), stop=(k == 15)),
                                     mark=(k == 15))
                    for k in range(16):
                        lh = w1buf[:, par * 4096 + k * 256 + jj * 128: par * 4096 + k * 256 + (jj + 1) * 128]
                        rh = xnTh[:, k * TT + tmv2 * MV: k * TT + (tmv2 + 1) * MV]
                        mmh = p.emit("tensor", lambda e, k=k, lh=lh, rh=rh, d=hp_[:, 0:MV]:
                                     e.matmul(d, lh, rh, start=(k == 0), stop=(k == 15)),
                                     mark=(k == 15))
                    ge_toks[(tmove, ei, jp)] = mmh
                    p.wait("scalar", mmg)
                    if kprev is not None and kprev in y_toks:
                        p.wait("scalar", y_toks[kprev])
                    gel = p.emit("scalar", lambda e, d=Gb[:, (j % 2) * 512:(j % 2) * 512 + MV],
                                 s=gp_[:, 0:MV], b=gbc[:, ei * 8 + j:ei * 8 + j + 1]:
                                 e.activation(d, s, AF.Gelu, bias=b))
                    ge_toks2[(tmove, ei, j, tmv2)] = gel
                    p.wait("vector", gel)
                    p.wait("vector", mmh)
                    y_toks[(tmove, ei, j, tmv2)] = p.v(
                        lambda e, d=Yb[:, j * TT + tmv2 * MV: j * TT + (tmv2 + 1) * MV],
                        s=hp_[:, 0:MV], b=b1c3[:, ei * 8 + j:ei * 8 + j + 1],
                        g2=Gb[:, (j % 2) * 512:(j % 2) * 512 + MV]:
                        e.scalar_tensor_tensor(d, s, b, g2, ALU.add, ALU.mult))
                if j % 2 == 1:
                    nj = jp + 2
                    if nj < 4:
                        emit_wload(ei, nj)
                    elif ei + 1 < n_exp:
                        emit_wload(ei + 1, nj - 4)
            for dm in range(4):
                src = c.wbW2.rearrange("(ej pp) dd -> pp ej dd", pp=128)[:, ei * 8:ei * 8 + 8,
                                                                        dm * 512:(dm + 1) * 512]
                par2 = dm % 2
                prev = w2mm_toks.get((tmove, ei, dm - 2)) or (
                    w2mm_toks.get((tmove, ei - 1, dm + 2)) if dm < 2 else None) or (
                    w2mm_toks.get((0, n_exp - 1, dm + 2)) if tmove == 1 and ei == 0 and dm < 2 else None)
                if prev is not None:
                    p.wait("sync", prev)
                w2ld = p.emit("sync", lambda e, d=w2buf[:, par2 * 4096:(par2 + 1) * 4096], s=src:
                              e.dma_start(out=d.rearrange("p (j dd) -> p j dd", j=8), in_=s.opt()),
                              dma=True)
                p.wait("tensor", w2ld)
                p.wait("tensor", y_toks[(tmove, ei, 7, TT // MV - 1)])
                for tb in range(NTB):
                    wp_ = c.pb[4 + tb % 2]
                    if evlast.get(tb % 2) is not None:
                        p.wait("tensor", evlast[tb % 2])
                    mmw = None
                    for j in range(8):
                        lh = Yb[:, j * TT + tb * 128: j * TT + (tb + 1) * 128]
                        rh = w2buf[:, par2 * 4096 + j * 512: par2 * 4096 + (j + 1) * 512]
                        mmw = p.emit("tensor", lambda e, j=j, lh=lh, rh=rh, d=wp_[:, :]:
                                     e.matmul(d, lh, rh, start=(j == 0), stop=(j == 7)),
                                     mark=(j == 7))
                    w2mm_toks[(tmove, ei, dm)] = mmw
                    p.wait("vector", mmw)
                    asl = acc[:, tb * 2048 + dm * 512: tb * 2048 + (dm + 1) * 512]
                    if ei < NRE:
                        csl = comb[:, tb * 16 + ei: tb * 16 + ei + 1]
                        if ei == 0:
                            ev_toks[(tmove, ei, dm, tb)] = p.v(
                                lambda e, d=asl, s=wp_[:, :], cs=csl:
                                e.tensor_scalar(d, s, cs, None, ALU.mult))
                            evlast[tb % 2] = ev_toks[(tmove, ei, dm, tb)]
                        else:
                            ev_toks[(tmove, ei, dm, tb)] = p.v(
                                lambda e, d=asl, s=wp_[:, :], cs=csl:
                                e.scalar_tensor_tensor(d, s, cs, d, ALU.mult, ALU.add))
                        evlast[tb % 2] = ev_toks[(tmove, ei, dm, tb)]
                    else:
                        ev_toks[(tmove, ei, dm, tb)] = p.v(
                            lambda e, d=asl, s=wp_[:, :]: e.tensor_add(d, s, d))
                    evlast[tb % 2] = ev_toks[(tmove, ei, dm, tb)]
        EXP_DONE = p.cur("vector")

        if dbg_ffn_only and "dumpcomb2" in dbg_skip:
            dv = p.cur("vector")
            p.wait("gpsimd", dv)
            if tmove == 0:
                for tb in range(NTB):
                    out_stores.append(p.emit("gpsimd", lambda e, tb=tb,
                                             s=acc[:, tb * 2048:(tb + 1) * 2048]:
                                             e.dma_start(out=c.out[tb * 128:(tb + 1) * 128, :], in_=s),
                                             dma=True))
                out_stores.append(p.emit("gpsimd", lambda e:
                                         e.dma_start(out=c.out[TT:TT + 128, 0:NTB * 16],
                                                     in_=comb[:, :]), dma=True))
            p.wait("sync", out_stores[-1])
            continue
        if dbg_ffn_only and "dumpcomb" in dbg_skip:
            dv = p.cur("vector")
            p.wait("gpsimd", dv)
            out_stores.append(p.emit("gpsimd", lambda e, tmove=tmove:
                                     e.dma_start(out=c.out[tmove * 128:(tmove + 1) * 128, 0:NTB * 16],
                                                 in_=comb[:, :]), dma=True))
            out_stores.append(p.emit("gpsimd", lambda e, tmove=tmove:
                                     e.dma_start(out=c.out[tmove * 128:(tmove + 1) * 128,
                                                           1024:1024 + 256],
                                                 in_=wstg3[:, 0:256]), dma=True))
            p.wait("sync", out_stores[-1])
            continue
        if dbg_ffn_only:
            dv = p.cur("vector")
            p.wait("gpsimd", dv)
            for tb in range(NTB):
                out_stores.append(p.emit("gpsimd", lambda e, tmove=tmove, tb=tb,
                                         s=acc[:, tb * 2048:(tb + 1) * 2048]:
                                         e.dma_start(out=c.out[tmove * TT + tb * 128:
                                                               tmove * TT + (tb + 1) * 128, :], in_=s),
                                         dma=True))
            p.wait("sync", out_stores[-1])
            continue
        rbev = {}
        for tb in (range(NTB) if "rb2" not in dbg_skip else ()):
            for dm in range(4):
                bk = 6 + (dm % 2)
                if (tb, dm - 2) in rbev:
                    p.wait("tensor", rbev[(tb, dm - 2)])
                elif (tb - 1, dm + 2) in rbev:
                    p.wait("tensor", rbev[(tb - 1, dm + 2)])
                mmt = p.emit("tensor", lambda e, tb=tb, dm=dm, bk=bk:
                             e.matmul(c.pb[bk][:, :], combT[:, tb * 128:(tb + 1) * 128],
                                      rb2s[:, dm * 512:(dm + 1) * 512], start=True, stop=True))
                p.wait("vector", mmt)
                rbev[(tb, dm)] = p.v(lambda e, d=acc[:, tb * 2048 + dm * 512: tb * 2048 + (dm + 1) * 512],
                                     s=c.pb[bk][:, :], bk=bk: e.tensor_add(d, s, d))

        if use_collective:
            p.wait("sync", c.CC_TOK)
        p.wait("tensor", p.cur("vector"))
        p.wait("sync", p.cur("tensor"))
        rl = []
        rl.append(p.emit("sync", lambda e: e.dma_start(out=scr3[0:1, 0:D], in_=c.postw_in[:]), dma=True))
        rl.append(p.emit("sync", lambda e: e.dma_start(out=ysf[0:1, 0:D], in_=c.postb_in[:]), dma=True))
        rl.append(p.emit("sync", lambda e: e.dma_start(out=sb2b[0:1, 0:D], in_=c.fsb2a[:]), dma=True))
        rl.append(p.emit("sync", lambda e: e.dma_start(out=pbrep[0:1, 0:D], in_=c.fsb2b[:]), dma=True))
        p.wait("vector", rl[-1])
        sb2tok = p.v(lambda e: e.tensor_add(sb2b[0:1, 0:D], sb2b[0:1, 0:D], pbrep[0:1, 0:D]))
        p.wait("tensor", sb2tok)
        for (dst, srow) in [(pwrep, scr3[0:1, 0:D]), (pbrep, ysf[0:1, 0:D]),
                            (None, sb2b[0:1, 0:D])]:
            for dm in range(4):
                mmt = p.emit("tensor", lambda e, d=c.pb[6][:, :], s=srow[0:1, dm * 512:(dm + 1) * 512]:
                             e.matmul(d, c.ones_sb[0:1, :], s, start=True, stop=True))
                p.wait("vector", mmt)
                dd = (dst if dst is not None else sb2b)
                bce = p.v(lambda e, d=dd[:, dm * 512:(dm + 1) * 512], s=c.pb[6][:, :]:
                          e.tensor_copy(d, s))
                p.wait("tensor", bce)
        TIP = min(2, NTB)
        ow_prev = None
        res_fin = None
        for tbp in range(NTB // TIP):
            ysT_toks = []
            for ti in range(TIP):
                tb = tbp * TIP + ti
                tglob = tbase + tb * 128
                if ti == 0 and res_fin is not None:
                    p.wait("sync", res_fin)
                for (rk, coff) in [(0, 0), (1, 1024)]:
                    rbase = rk * L
                    l0 = p.emit("sync", lambda e, d=xrs[:, 0:1024],
                                s=c.ccout[rbase + tglob: rbase + tglob + 128, :]:
                                e.dma_start(out=d, in_=s), dma=True)
                    l1 = p.emit("sync", lambda e, d=xrs[:, 1024:2048],
                                s=c.ccout[rbase + TOWN + tglob: rbase + TOWN + tglob + 128, :]:
                                e.dma_start(out=d, in_=s), dma=True)
                    p.wait("vector", l1)
                    p.v(lambda e: e.tensor_sub(xrs[:, 0:1024], xrs[:, 0:1024], xrs[:, 1024:2048]))
                    bl = p.v(lambda e, d=ysf[:, coff:coff + 1024]:
                             e.scalar_tensor_tensor(d, xrs[:, 0:1024], c.m0c[:, 0:1],
                                                    xrs[:, 1024:2048], ALU.mult, ALU.add))
                    p.wait("sync", bl)
                p.v(lambda e: e.reduce_sum(colz[:, 4:5], ysf[:], axis=AX.X))
                p.v(lambda e: e.tensor_single_scalar(colz[:, 5:6], colz[:, 4:5], -1.0 / D, ALU.mult))
                p.wait("scalar", p.cur("vector"))
                ua = p.emit("scalar", lambda e: e.add(scr3[:], ysf[:], colz[:, 5:6]))
                p.wait("scalar", ua)
                sqa = p.emit("scalar", lambda e: e.activation(ysf[:], scr3[:], AF.Square,
                                                              accum_out=colz[:, 6:7]))
                p.wait("vector", sqa)
                p.v(lambda e: e.tensor_scalar(colz[:, 7:8], colz[:, 6:7], 1.0 / D, EPST,
                                              ALU.mult, ALU.add))
                emit_rsqrt(p, colz[:, 7:8], colz[:, 8:9], colz[:, 9:10])
                p.v(lambda e: e.tensor_scalar(scr3[:], scr3[:], colz[:, 8:9], None, ALU.mult))
                p.v(lambda e: e.tensor_mul(scr3[:], scr3[:], pwrep[:]))
                nrm = p.v(lambda e, d=ysn[:, ti * 2048:(ti + 1) * 2048]:
                          e.tensor_add(d, scr3[:], pbrep[:]))
                p.wait("tensor", nrm)
                for k in range(16):
                    tsl = 2 + (k % 2)
                    trp = p.emit("tensor", lambda e, k=k, ti=ti, tsl=tsl:
                                 e.transpose(c.pbB[tsl][:, 0:128],
                                             ysn[:, ti * 2048 + k * 128: ti * 2048 + (k + 1) * 128],
                                             c.idb[:]))
                    p.wait("vector", trp)
                    tv = p.v(lambda e, k=k, ti=ti, tsl=tsl:
                             e.tensor_copy(ysnT[:, (ti * 16 + k) * 128:(ti * 16 + k + 1) * 128],
                                           c.pbB[tsl][:, 0:128]))
                    p.wait("tensor", tv)
                ysT_toks.append(p.cur("vector"))
            p.wait("tensor", ysT_toks[-1])
            mmo = None
            ow_dm = []
            for dm in range(4):
                ob = owbuf0 if dm % 2 == 0 else owbuf1
                if dm >= 2:
                    p.wait("sync", ow_dm[dm - 2])
                elif ow_prev is not None:
                    p.wait("sync", ow_prev[dm + 2])
                src = c.wbO.rearrange("(k pp) dd -> pp k dd", pp=128)[:, :, dm * 512:(dm + 1) * 512]
                owl = p.emit("sync", lambda e, d=ob, s=src:
                             e.dma_start(out=d.rearrange("p (k dd) -> p k dd", k=16), in_=s.opt()),
                             dma=True)
                p.wait("tensor", owl)
                for ti in range(TIP):
                    for k in range(16):
                        mmo = p.emit("tensor", lambda e, k=k, ti=ti, dm=dm, ob=ob:
                                     e.matmul(c.pb[ti * 4 + dm][:, :],
                                              ysnT[:, (ti * 16 + k) * 128:(ti * 16 + k + 1) * 128],
                                              ob[:, k * 512:(k + 1) * 512],
                                              start=(k == 0), stop=(k == 15)),
                                     mark=(k == 15))
                ow_dm.append(mmo)
            ow_prev = list(ow_dm)
            p.wait("vector", mmo)
            for ti in range(TIP):
                tb = tbp * TIP + ti
                tglob = tbase + tb * 128
                if res_fin is not None:
                    p.wait("sync", res_fin)
                xl = p.emit("sync", lambda e, d=xrs[:, 0:2048],
                            s=c.x_own[tmove * TT + tb * 128: tmove * TT + (tb + 1) * 128, :]:
                            e.dma_start(out=d, in_=s), dma=True)
                x0 = p.emit("sync", lambda e, d=scr3[:],
                            s=c.xn_tmD[tglob:tglob + 128, :]: e.dma_start(out=d, in_=s), dma=True)
                x1 = p.emit("sync", lambda e, d=xrs[:, 2048:4096],
                            s=c.xn_tmD[TOWN + tglob: TOWN + tglob + 128, :]:
                            e.dma_start(out=d, in_=s), dma=True)
                p.wait("vector", x1)
                p.wait("vector", xl)
                p.v(lambda e: e.tensor_sub(scr3[:], scr3[:], xrs[:, 2048:4096]))
                p.v(lambda e: e.scalar_tensor_tensor(scr3[:], scr3[:], c.m0c[:, 0:1],
                                                     xrs[:, 2048:4096], ALU.mult, ALU.add))
                for dm in range(4):
                    p.v(lambda e, d=acc[:, tb * 2048 + dm * 512: tb * 2048 + (dm + 1) * 512],
                        s=c.pb[ti * 4 + dm][:, :]: e.tensor_add(d, s, d))
                p.v(lambda e, d=acc[:, tb * 2048:(tb + 1) * 2048]:
                    e.tensor_add(d, d, xrs[:, 0:2048]))
                p.v(lambda e, d=acc[:, tb * 2048:(tb + 1) * 2048], s=scr3[:]:
                    e.tensor_add(d, d, s))
                fin = p.v(lambda e, d=acc[:, tb * 2048:(tb + 1) * 2048]:
                          e.tensor_add(d, d, sb2b[:]))
                res_fin = fin
                p.wait("gpsimd", fin)
                out_stores.append(p.emit("gpsimd", lambda e, tmove=tmove, tb=tb,
                                         s=acc[:, tb * 2048:(tb + 1) * 2048]:
                                         e.dma_start(out=c.out[tmove * TT + tb * 128:
                                                               tmove * TT + (tb + 1) * 128, :], in_=s),
                                         dma=True))
    emit_final_waits(c)
    sb3.close()


def pack_inputs(inp, core, L=4096, n_exp=NE):
    b = core // 2
    half = core % 2
    TOWN = L // 2
    x = np.ascontiguousarray(inp["L0_attn_hidden_states"][b])
    h0 = half * NHC
    heads = slice(h0, h0 + NHC)

    def T(a):
        return np.ascontiguousarray(np.transpose(a))

    qw = T(inp["L0_attn_q_w"])[:, h0 * HD:(h0 + NHC) * HD]
    kw = T(inp["L0_attn_k_w"])[:, h0 * HD:(h0 + NHC) * HD]
    vw = T(inp["L0_attn_v_w"])[:, h0 * HD:(h0 + NHC) * HD]
    ow = T(inp["L0_attn_o_w"])
    lrw = T(inp["L0_attn_ttt_lr_weight"][heads, 0, :])
    lrb = np.ascontiguousarray(inp["L0_attn_ttt_lr_bias"][heads])

    W1 = inp["L0_attn_W1"][heads]
    wst = np.zeros((128, 512), np.float32)
    for g in range(2):
        for pr in range(4):
            for a in range(2):
                hloc = 8 * g + 2 * pr + a
                wst[a * 64:(a + 1) * 64, (g * 4 + pr) * 64:(g * 4 + pr + 1) * 64] = W1[hloc]
    b1 = inp["L0_attn_b1"][heads, 0, :]
    b1t = np.zeros((128, 128), np.float32)
    gm = inp["L0_attn_ttt_norm_weight"][heads]
    bt = inp["L0_attn_ttt_norm_bias"][heads]
    gmt = np.zeros((128, 128), np.float32)
    btt = np.zeros((128, 128), np.float32)
    for g in range(2):
        for cc in range(8):
            hloc = 8 * g + cc
            rows = slice(16 * cc, 16 * cc + 16)
            b1t[rows, 64 * g:64 * g + 64] = b1[hloc]
            gmt[rows, 64 * g:64 * g + 64] = gm[hloc]
            btt[rows, 64 * g:64 * g + 64] = bt[hloc]
    lti = inp["L0_attn_learnable_token_idx"]
    lti_col = np.tile(lti.reshape(1, 16), (8, 1)).reshape(128, 1)
    lti15 = np.broadcast_to(lti[15:16], (128,)).reshape(128, 1).copy()
    invidx = np.tile((1.0 / np.arange(1, 17, dtype=np.float32)).reshape(1, 16), (8, 1)).reshape(128, 1)
    mk = np.zeros((128, 64), np.float32)
    tri = np.triu(np.ones((16, 16), np.float32))
    for pr in range(4):
        for g in range(2):
            mk[32 * pr:32 * pr + 16, 32 * g:32 * g + 16] = tri
            mk[32 * pr + 16:32 * pr + 32, 32 * g + 16:32 * g + 32] = tri
    onesbd = np.zeros((32, 32), np.float32)
    onesbd[0:16, 0:16] = 1.0
    onesbd[16:32, 16:32] = 1.0
    onesbd = np.tile(onesbd, (4, 1))
    bdm = np.zeros((128, 128), np.float32)
    for pr in range(4):
        bdm[32 * pr:32 * pr + 16, 0:64] = 1.0
        bdm[32 * pr + 16:32 * pr + 32, 64:128] = 1.0
    bdmask_rep = np.tile(bdm, (1, 16))
    import ml_dtypes
    bf = lambda a: a.astype(ml_dtypes.bfloat16)
    ident = np.eye(128, dtype=np.float32)
    m0 = np.full((128, 1), 1.0 - half, np.float32)

    wa = inp["L0_ffn_wa"]
    rg, rw1, rw2 = inp["L0_ffn_rg"], inp["L0_ffn_rw1"], inp["L0_ffn_rw2"]
    sg, sw1, sw2 = inp["L0_ffn_sg"], inp["L0_ffn_sw1"], inp["L0_ffn_sw2"]
    f_g = np.concatenate([rg.reshape(NRE * D, FH), sg.reshape(NSH * D, FH)], 0)
    f_w1 = np.concatenate([rw1.reshape(NRE * D, FH), sw1.reshape(NSH * D, FH)], 0)
    f_w2 = np.concatenate([rw2.reshape(NRE * FH, D), sw2.reshape(NSH * FH, D)], 0)
    rgb = np.concatenate([inp["L0_ffn_rgb"], inp["L0_ffn_sgb"]], 0)
    rb1 = np.concatenate([inp["L0_ffn_rb1"], inp["L0_ffn_sb1"]], 0)
    ne_all = rgb.shape[0]
    f_gb = np.ascontiguousarray(rgb.reshape(ne_all * 8, 128).T)
    f_b1 = np.ascontiguousarray(rb1.reshape(ne_all * 8, 128).T)

    im = {
        "x_b": x,
        "x_own": np.ascontiguousarray(x[half * TOWN:(half + 1) * TOWN]),
        "qw_t": np.ascontiguousarray(qw), "kw_t": np.ascontiguousarray(kw),
        "vw_t": np.ascontiguousarray(vw), "ow_t": np.ascontiguousarray(ow),
        "lrw_t": np.ascontiguousarray(lrw), "lrb": lrb,
        "w1_init": wst, "b1_init": b1t, "gam_t": gmt, "bet_t": btt,
        "lti_col": lti_col, "lti15_col": lti15, "invidx_col": invidx,
        "masketa": mk, "onesbd": bf(onesbd), "bdmask_rep": bf(bdmask_rep),
        "ident_bf": bf(ident), "ident_f": ident, "m0col": m0,
        "m0mask_b": bf(np.full((128, 1024), 1.0 - half, np.float32)),
        "m0mask_f": np.full((128, 2048), 1.0 - half, np.float32),
        "postw_row": inp["L0_attn_post_norm_weight"].reshape(1, D),
        "postb_row": inp["L0_attn_post_norm_bias"].reshape(1, D),
        "onesrow": np.ones((1, 128), np.float32),
        "wa_t": np.ascontiguousarray(wa),
        "f_g": f_g, "f_w1": f_w1, "f_w2": f_w2,
        "f_gb": f_gb, "f_b1": f_b1,
        "f_rb2": np.concatenate([inp["L0_ffn_rb2"],
                                 np.zeros((128 - NRE, D), np.float32)], 0),
        "f_sb2a": inp["L0_ffn_sb2"][0:1], "f_sb2b": inp["L0_ffn_sb2"][1:2],
    }
    return {k: np.ascontiguousarray(v) for k, v in im.items()}


def kernel(**inputs):
    L = inputs["L0_attn_hidden_states"].shape[1]
    nc = build_nc(L=L, phases=3)
    in_maps = [pack_inputs(inputs, core, L=L) for core in range(8)]
    res = run_bass_kernel_spmd(nc, in_maps, core_ids=list(range(8)))
    TOWN = L // 2
    out = np.zeros((4, L, D), np.float32)
    for core in range(8):
        b, half = core // 2, core % 2
        out[b, half * TOWN:(half + 1) * TOWN] = res.results[core]["out"]
    return out


# version 06
# speedup vs baseline: 1.0077x; 1.0077x over previous
import contextlib
from types import SimpleNamespace as NS
import numpy as np
import concourse.bass as bass
import concourse.mybir as mybir
from concourse.bass_utils import run_bass_kernel_spmd

F32 = mybir.dt.float32
BF16 = mybir.dt.bfloat16
I32 = mybir.dt.int32
AF = mybir.ActivationFunctionType
ALU = mybir.AluOpType
AX = mybir.AxisListType

D = 2048
NHC = 16
HD = 64
K = 16
FH = 1024
NRE = 16
NSH = 2
NE = NRE + NSH
EPS0 = 1e-5
EPST = 1e-6
MAGIC = 0x5F3759DF
ENGS = ("sync", "vector", "scalar", "tensor", "gpsimd")


class Prog:

    def __init__(self, nc, es):
        self.nc = nc
        self.q = {e: [] for e in ENGS}
        self.sems = {}
        for e in ENGS:
            self.sems[e] = [es.enter_context(nc.semaphore("s_" + e)), 0]
            self.sems["d" + e] = [es.enter_context(nc.semaphore("d_" + e)), 0]

    def emit(self, eng, fn, mark=True, dma=False):
        key = ("d" + eng) if dma else eng
        rec = self.sems[key]
        if mark:
            rec[1] += 16 if dma else 1

            def wrapped(e, fn=fn, rec=rec, k=16 if dma else 1):
                fn(e).then_inc(rec[0], k)
            self.q[eng].append(wrapped)
        else:
            self.q[eng].append(fn)
        return (key, rec[1])

    def wait(self, eng, token):
        key, val = token
        rec = self.sems[key]
        if val > 0:
            self.q[eng].append(lambda e, r=rec, v=val: e.wait_ge(r[0], v))

    def cur(self, eng):
        return (eng, self.sems[eng][1])

    def v(self, fn):
        t = self.emit("vector", fn)
        self.wait("vector", t)
        return t

    def run(self, block):
        for eng in ENGS:
            def make(eng):
                def body(e):
                    for fn in self.q[eng]:
                        fn(e)
                return body
            getattr(block, eng)(make(eng))


def emit_rsqrt(p, vcol, ycol, tcol, iters=2):
    yb = ycol.bitcast(I32)
    vb = vcol.bitcast(I32)
    p.v(lambda e: e.tensor_single_scalar(yb, vb, 1, ALU.logical_shift_right))
    p.v(lambda e: e.tensor_single_scalar(yb, yb, MAGIC, ALU.subtract))
    p.v(lambda e: e.tensor_single_scalar(yb, yb, -1, ALU.mult))
    for _ in range(iters):
        p.v(lambda e: e.tensor_mul(tcol, ycol, ycol))
        p.v(lambda e: e.scalar_tensor_tensor(tcol, tcol, -0.5, vcol, ALU.mult, ALU.mult))
        p.v(lambda e: e.scalar_tensor_tensor(ycol, tcol, 1.5, ycol, ALU.add, ALU.mult))
    return p.cur("vector")


def make_ctx(L, n_exp):
    c = NS()
    c.L = L
    c.n_exp = n_exp
    c.NM = L // K
    c.TOWN = L // 2
    c.NT = L // 128
    c.NTO = c.TOWN // 128
    c.NCHK = c.NM // 16
    c.NPC = L // 512
    c.nc = bass.Bass("TRN2", target_bir_lowering=False, debug=False)
    c.es = contextlib.ExitStack()
    c.p = Prog(c.nc, c.es)
    return c


def declare_io(c):
    nc, L, TOWN, n_exp = c.nc, c.L, c.TOWN, c.n_exp

    def din(name, shape, dtype=F32):
        return nc.dram_tensor(name, shape, dtype, kind="ExternalInput").ap()

    c.x_in = din("x_b", [L, D])
    c.x_own = din("x_own", [TOWN, D])
    c.qw = din("qw_t", [D, NHC * HD])
    c.kw = din("kw_t", [D, NHC * HD])
    c.vw = din("vw_t", [D, NHC * HD])
    c.ow = din("ow_t", [D, D])
    c.lrw = din("lrw_t", [D, NHC])
    c.lrb = din("lrb", [NHC, 1])
    c.w1_0 = din("w1_init", [128, 512])
    c.b1_0 = din("b1_init", [128, 128])
    c.gam_in = din("gam_t", [128, 128])
    c.bet_in = din("bet_t", [128, 128])
    c.lti_c = din("lti_col", [128, 1])
    c.lti15 = din("lti15_col", [128, 1])
    c.invi_c = din("invidx_col", [128, 1])
    c.mketa = din("masketa", [128, 64])
    c.onesbd = din("onesbd", [128, 32], BF16)
    c.bdmask_rep = din("bdmask_rep", [128, 2048], BF16)
    c.ident_bf = din("ident_bf", [128, 128], BF16)
    c.ident_f = din("ident_f", [128, 128])
    c.m0_in = din("m0col", [128, 1])
    c.m0mb = din("m0mask_b", [128, 1024], BF16)
    c.m0mf = din("m0mask_f", [128, 2048])
    c.postw_in = din("postw_row", [1, D])
    c.postb_in = din("postb_row", [1, D])
    c.onesrow_in = din("onesrow", [1, 128])
    c.wa_in = din("wa_t", [D, NRE])
    c.fg = din("f_g", [n_exp * 16 * 128, FH])
    c.fw1 = din("f_w1", [n_exp * 16 * 128, FH])
    c.fw2 = din("f_w2", [n_exp * 8 * 128, D])
    c.fgb = din("f_gb", [128, n_exp * 8])
    c.fb1 = din("f_b1", [128, n_exp * 8])
    c.frb2 = din("f_rb2", [128, D])
    c.fsb2a = din("f_sb2a", [1, D])
    c.fsb2b = din("f_sb2b", [1, D])

    c.out = nc.dram_tensor("out", [TOWN, D], F32, kind="ExternalOutput").ap()

    def dsc(name, shape, dtype=F32):
        return nc.dram_tensor(name, shape, dtype, kind="Internal").ap()

    c.xn_tmD = dsc("xn_tmD", [L, D])
    c.xnT_D = dsc("xnT_D", [D, L], BF16)
    c.kfmD = dsc("kfmD", [NHC * HD, L], BF16)
    c.qfmD = dsc("qfmD", [NHC * HD, L], BF16)
    c.qtmD = dsc("qtmD", [L, NHC * HD])
    c.ktmD = dsc("ktmD", [L, NHC * HD])
    c.vtmD = dsc("vtmD", [L, NHC * HD])
    c.siglrD = dsc("siglrD", [c.NM, 256])
    c.ysD = dsc("ysD", [L, NHC * HD])
    c.ccout = dsc("ccout", [2 * L, NHC * HD])
    c.wbG = dsc("wbG", [n_exp * 16 * 128, FH], BF16)
    c.wbW1 = dsc("wbW1", [n_exp * 16 * 128, FH], BF16)
    c.wbW2 = dsc("wbW2", [n_exp * 8 * 128, D], BF16)
    c.wbO = dsc("wbO", [D, D], BF16)

    c.pb = [c.es.enter_context(nc.psum_tensor(f"pb{i}", [128, 512], F32)) for i in range(8)]
    c.pbB = [t.ap().bitcast(BF16) for t in c.pb]

    c.idb = c.es.enter_context(nc.sbuf_tensor("idb", [128, 128], BF16))
    c.idf = c.es.enter_context(nc.sbuf_tensor("idf", [128, 128], F32))
    c.m0c = c.es.enter_context(nc.sbuf_tensor("m0c", [128, 1], F32))
    c.ones_sb = c.es.enter_context(nc.sbuf_tensor("ones_sb", [1, 128], F32))
    p = c.p
    p.emit("sync", lambda e: e.dma_start(out=c.idb[:], in_=c.ident_bf[:]), dma=True)
    p.emit("sync", lambda e: e.dma_start(out=c.idf[:], in_=c.ident_f[:]), dma=True)
    p.emit("sync", lambda e: e.dma_start(out=c.m0c[:], in_=c.m0_in[:]), dma=True)
    c.CONST_LD = p.emit("sync", lambda e: e.dma_start(out=c.ones_sb[:], in_=c.onesrow_in[:]), dma=True)


def emit_phase1(c):
    p, nc, L = c.p, c.nc, c.L
    WCOL = 3 * NHC * HD + 16
    sb1 = contextlib.ExitStack()
    wqkv = sb1.enter_context(nc.sbuf_tensor("wqkv", [128, 16 * WCOL], BF16))
    xtile = sb1.enter_context(nc.sbuf_tensor("xtile", [128, 2 * D], F32))
    xnf = sb1.enter_context(nc.sbuf_tensor("xnf", [128, 2 * D], F32))
    xnb = sb1.enter_context(nc.sbuf_tensor("xnb", [128, 2 * D], BF16))
    xnTc = sb1.enter_context(nc.sbuf_tensor("xnTc", [128, 16 * 512], BF16))
    lncol = sb1.enter_context(nc.sbuf_tensor("lncol", [128, 8], F32))
    trevac = sb1.enter_context(nc.sbuf_tensor("trevac", [128, 2 * 16 * 128], BF16))
    pevac = sb1.enter_context(nc.sbuf_tensor("pevac", [128, 4 * 512], BF16))
    tfb = sb1.enter_context(nc.sbuf_tensor("tfb", [128, 8 * 128], F32))
    sigev = sb1.enter_context(nc.sbuf_tensor("sigev", [16, 2 * 512], F32))
    lrbc = sb1.enter_context(nc.sbuf_tensor("lrbc", [16, 1], F32))
    stg = [sb1.enter_context(nc.sbuf_tensor(f"stg{i}", [128, 1024], F32)) for i in range(2)]
    wstg = [sb1.enter_context(nc.sbuf_tensor("wstg0", [128, D], F32))] * 2
    cbf = sb1.enter_context(nc.sbuf_tensor("cbf", [128, 2 * 2048], BF16))

    lrb_ld = p.emit("sync", lambda e: e.dma_start(out=lrbc[:], in_=c.lrb[:]), dma=True)

    wsrcs = [(c.qw, 0, NHC * HD), (c.kw, NHC * HD, NHC * HD), (c.vw, 2 * NHC * HD, NHC * HD),
             (c.lrw, 3 * NHC * HD, NHC)]
    wcasts = []
    i = 0
    for kk in range(16):
        for (src, coff, w) in wsrcs:
            dst = wqkv[:, kk * WCOL + coff: kk * WCOL + coff + w]
            sap = src[kk * 128:(kk + 1) * 128, :]
            if i >= 2:
                p.wait("sync", wcasts[i - 2])
            cld = p.emit("sync", lambda e, d=stg[i % 2], s=sap, w=w:
                         e.dma_start(out=d[:, 0:w], in_=s), dma=True)
            p.wait("scalar", cld)
            wcasts.append(p.emit("scalar", lambda e, d=dst, s=stg[i % 2], w=w: e.copy(d, s[:, 0:w])))
            i += 1
    c.WCAST_LAST = wcasts[-1]

    cast_jobs = []
    for e_i in range(c.n_exp):
        for kk in range(16):
            r = (e_i * 16 + kk) * 128
            cast_jobs.append((c.fg[r:r + 128, :], c.wbG[r:r + 128, :], FH))
            cast_jobs.append((c.fw1[r:r + 128, :], c.wbW1[r:r + 128, :], FH))
        for jj in range(8):
            r = (e_i * 8 + jj) * 128
            cast_jobs.append((c.fw2[r:r + 128, :], c.wbW2[r:r + 128, :], D))
    for kk in range(16):
        cast_jobs.append((c.ow[kk * 128:(kk + 1) * 128, :], c.wbO[kk * 128:(kk + 1) * 128, :], D))
    fw_store = []
    for i, (src, dst, w) in enumerate(cast_jobs):
        if i >= 1:
            p.wait("gpsimd", fw_store[i - 1])
        cld = p.emit("gpsimd", lambda e, d=wstg[i % 2], s=src, w=w:
                     e.dma_start(out=d[:, 0:w], in_=s), dma=True)
        p.wait("gpsimd", cld)
        ccp = p.emit("gpsimd", lambda e, d=cbf[:, (i % 2) * 2048:(i % 2) * 2048 + 2048],
                     s=wstg[i % 2], w=w: e.tensor_copy(d[:, 0:w], s[:, 0:w]))
        p.wait("gpsimd", ccp)
        fw_store.append(
            p.emit("gpsimd", lambda e, d=dst, s=cbf[:, (i % 2) * 2048:(i % 2) * 2048 + 2048], w=w:
                   e.dma_start(out=d, in_=s[:, 0:w]), dma=True))
    c.FW_LAST = fw_store[-1]

    sq_done = []
    xn_store = []
    xnt_store = []
    prev_ev = None
    for it in range(c.NT):
        par = it % 2
        xt = xtile[:, par * D:(par + 1) * D]
        xn_f = xnf[:, par * D:(par + 1) * D]
        xn_b = xnb[:, par * D:(par + 1) * D]
        if it >= 2:
            p.wait("sync", sq_done[it - 2])
            p.wait("sync", xnt_store[it - 2])
        xldc = p.emit("sync", lambda e, d=xt, s=c.x_in[it * 128:(it + 1) * 128, :]:
                      e.dma_start(out=d, in_=s), dma=True)
        p.wait("vector", xldc)
        p.v(lambda e, xt=xt: e.reduce_sum(lncol[:, 0:1], xt, axis=AX.X))
        muc = p.v(lambda e: e.tensor_single_scalar(lncol[:, 1:2], lncol[:, 0:1], -1.0 / D, ALU.mult))
        p.wait("scalar", muc)
        if it >= 2:
            p.wait("scalar", xn_store[it - 2])
        uc = p.emit("scalar", lambda e, d=xn_f, s=xt: e.add(d, s, lncol[:, 1:2]))
        p.wait("scalar", uc)
        sq_done.append(p.emit("scalar", lambda e, s=xn_f, d=xt:
                              e.activation(d, s, AF.Square, accum_out=lncol[:, 2:3])))
        p.wait("vector", sq_done[-1])
        p.v(lambda e: e.tensor_scalar(lncol[:, 3:4], lncol[:, 2:3], 1.0 / D, EPS0, ALU.mult, ALU.add))
        emit_rsqrt(p, lncol[:, 3:4], lncol[:, 4:5], lncol[:, 5:6])
        p.v(lambda e, d=xn_f: e.tensor_scalar(d, d, lncol[:, 4:5], None, ALU.mult))
        nbc = p.v(lambda e, d=xn_b, s=xn_f: e.tensor_copy(d, s))
        p.wait("gpsimd", nbc)
        xn_store.append(p.emit("gpsimd", lambda e, d=c.xn_tmD[it * 128:(it + 1) * 128, :], s=xn_f:
                               e.dma_start(out=d, in_=s), dma=True))
        p.wait("tensor", nbc)
        ev = []
        for bk in range(16):
            slot = bk % 2
            if bk >= 2:
                p.wait("tensor", ev[bk - 2])
            elif prev_ev is not None:
                p.wait("tensor", prev_ev[14 + bk])
            trc = p.emit("tensor", lambda e, d=c.pbB[slot][:, 0:128],
                         s=xn_b[:, bk * 128:(bk + 1) * 128]: e.transpose(d, s, c.idb[:]))
            p.wait("vector", trc)
            ev.append(p.emit("vector", lambda e,
                             d=trevac[:, (par * 16 + bk) * 128:(par * 16 + bk + 1) * 128],
                             s=c.pbB[slot][:, 0:128]: e.tensor_copy(d, s)))
        prev_ev = ev
        p.wait("gpsimd", ev[-1])
        xnt_store.append(
            p.emit("gpsimd", lambda e, it=it, par=par:
                   e.dma_start(out=c.xnT_D.rearrange("(k p) t -> p k t", p=128)[:, :, it * 128:(it + 1) * 128],
                               in_=trevac[:, par * 2048:(par + 1) * 2048].rearrange("p (k t) -> p k t", k=16)),
                   dma=True))
    c.XN_LAST = xn_store[-1]
    c.XNT_LAST = xnt_store[-1]

    OFC = 3 * NHC * HD // 128
    p.wait("tensor", c.WCAST_LAST)
    fm_store, tm_store, sg_store, pjev, tmev, lr_mm = [], [], [], [], [], []
    for ch in range(c.NPC):
        t0 = ch * 512
        p.wait("sync", c.XNT_LAST)
        if ch >= 1:
            p.wait("sync", lr_mm[-1])
        xtl = p.emit("sync", lambda e, t0=t0:
                     e.dma_start(out=xnTc[:].rearrange("p (k t) -> p k t", k=16),
                                 in_=c.xnT_D.rearrange("(k p) t -> p k t", p=128)[:, :, t0:t0 + 512]),
                     dma=True)
        p.wait("tensor", xtl)
        for oc in range(OFC):
            gidx = ch * OFC + oc
            slot = oc % 2
            if gidx >= 2:
                p.wait("tensor", pjev[gidx - 2])
            mmc = None
            for kk in range(16):
                mmc = p.emit("tensor", lambda e, kk=kk, oc=oc, slot=slot:
                             e.matmul(c.pb[slot][:, :],
                                      wqkv[:, kk * WCOL + oc * 128:kk * WCOL + (oc + 1) * 128],
                                      xnTc[:, kk * 512:(kk + 1) * 512],
                                      start=(kk == 0), stop=(kk == 15)),
                             mark=(kk == 15))
            p.wait("vector", mmc)
            if gidx >= 4:
                p.wait("vector", tr_last[gidx - 4])
                if fmst_of[gidx - 4] is not None:
                    p.wait("vector", fmst_of[gidx - 4])
            if gidx == 0:
                tr_last, fmst_of = {}, {}
            pjev.append(p.emit("vector", lambda e, d=pevac[:, (oc % 4) * 512:(oc % 4) * 512 + 512],
                               s=c.pb[slot][:, :]: e.tensor_copy(d, s)))
            src_t = ["q", "k", "v"][oc // 8]
            fr = (oc % 8) * 128
            if src_t in ("k", "q"):
                dstD = c.kfmD if src_t == "k" else c.qfmD
                p.wait("gpsimd", pjev[-1])
                fm_store.append(
                    p.emit("gpsimd", lambda e, d=dstD[fr:fr + 128, t0:t0 + 512],
                           s=pevac[:, (oc % 4) * 512:(oc % 4) * 512 + 512]:
                           e.dma_start(out=d, in_=s), dma=True))
                fmst_of[gidx] = fm_store[-1]
            else:
                fmst_of[gidx] = None
            dstT = {"q": c.qtmD, "k": c.ktmD, "v": c.vtmD}[src_t]
            p.wait("tensor", pjev[-1])
            for b4 in range(4):
                gt = len(tmev)
                tslot = 2 + (gt % 2)
                if gt >= 2:
                    p.wait("tensor", tmev[gt - 2])
                trc = p.emit("tensor", lambda e, d=c.pbB[tslot][:, 0:128],
                             s=pevac[:, (oc % 4) * 512 + b4 * 128:(oc % 4) * 512 + (b4 + 1) * 128]:
                             e.transpose(d, s, c.idb[:]))
                p.wait("vector", trc)
                if gt >= 8:
                    p.wait("vector", tm_store[gt - 8])
                tmev.append(p.emit("vector", lambda e, d=tfb[:, (gt % 8) * 128:(gt % 8) * 128 + 128],
                                   s=c.pbB[tslot][:, 0:128]: e.tensor_copy(d, s)))
                p.wait("gpsimd", tmev[-1])
                tm_store.append(
                    p.emit("gpsimd", lambda e, d=dstT[t0 + b4 * 128:t0 + (b4 + 1) * 128, fr:fr + 128],
                           s=tfb[:, (gt % 8) * 128:(gt % 8) * 128 + 128]:
                           e.dma_start(out=d, in_=s), dma=True))
            tr_last[gidx] = trc
        if ch >= 1:
            p.wait("tensor", sgc)
        mmc = None
        for kk in range(16):
            mmc = p.emit("tensor", lambda e, kk=kk:
                         e.matmul(c.pb[4][0:16, :],
                                  wqkv[:, kk * WCOL + 3 * NHC * HD:kk * WCOL + 3 * NHC * HD + 16],
                                  xnTc[:, kk * 512:(kk + 1) * 512],
                                  start=(kk == 0), stop=(kk == 15)),
                         mark=(kk == 15))
        lr_mm.append(mmc)
        p.wait("scalar", lrb_ld)
        p.wait("scalar", mmc)
        if ch >= 2:
            p.wait("scalar", sg_store[ch - 2])
        sgc = p.emit("scalar", lambda e, d=sigev[:, (ch % 2) * 512:(ch % 2) * 512 + 512],
                     s=c.pb[4][0:16, :]: e.activation(d, s, AF.Sigmoid, bias=lrbc[:, 0:1]))
        p.wait("gpsimd", sgc)
        sg_store.append(
            p.emit("gpsimd", lambda e, ch=ch:
                   e.dma_start(out=c.siglrD.rearrange("n (h s) -> h n s", s=16)[:, ch * 32:(ch + 1) * 32, :],
                               in_=sigev[:, (ch % 2) * 512:(ch % 2) * 512 + 512]
                               .rearrange("h (n s) -> h n s", s=16)),
                   dma=True))
    c.P1_TM_LAST = tm_store[-1]
    c.P1_FM_LAST = fm_store[-1]
    c.P1_SG_LAST = sg_store[-1]
    c.P1_PJ_LAST = lr_mm[-1]
    sb1.close()



def emit_phase2(c):
    p, nc, L, NM, NCHK = c.p, c.nc, c.L, c.NM, c.NCHK
    sb2 = contextlib.ExitStack()
    sbt = lambda n, sh, dt=F32: sb2.enter_context(nc.sbuf_tensor(n, sh, dt))
    fmk = sbt("fmk", [128, 2 * 8 * 256], BF16)
    fmq = sbt("fmq", [128, 2 * 8 * 256], BF16)
    tmq = sbt("tmq", [128, 2 * 2048])
    tmk = sbt("tmk", [128, 2 * 2048])
    tmv = sbt("tmv", [128, 2 * 2048])
    bdk = sbt("bdk", [128, 2 * 8 * 16 * 32], BF16)
    bdq = sbt("bdq", [128, 2 * 8 * 16 * 32], BF16)
    bdxk = sbt("bdxk", [128, 2 * 2 * 2048], BF16)
    bdmr = sbt("bdmr", [128, 2048], BF16)
    sigrow = sbt("sigrow", [1, 2 * 4096])
    yschunk = sbt("yschunk", [128, 2 * 2048])
    wst = sbt("wst", [128, 512])
    wstb = sbt("wstb", [128, 512], BF16)
    b1st = sbt("b1st", [128, 128])
    gam = sbt("gam", [128, 128])
    bet = sbt("bet", [128, 128])
    gam2 = sbt("gam2", [128, 128])
    gbt = sbt("gbt", [128, 128])
    mket = sbt("mket", [128, 64])
    ones128 = sbt("ones128", [128, 32], BF16)
    tokc = sbt("tokc", [128, 4])
    mtt = sbt("mtt", [128, 64], BF16)
    ue = sbt("ue", [128, 128])
    xh = sbt("xh", [128, 128])
    gxh = sbt("gxh", [128, 128])
    sc1 = sbt("sc1", [128, 128])
    sc2 = sbt("sc2", [128, 128])
    ctt = sbt("ctt", [128, 128])
    zbar = sbt("zbar", [128, 128])
    grad = sbt("gradp", [128, 128], BF16)
    cols = sbt("cols", [128, 16])

    p.wait("sync", c.P1_PJ_LAST)
    p.wait("sync", c.P1_TM_LAST)
    p.wait("sync", c.P1_FM_LAST)
    p.wait("sync", c.P1_SG_LAST)
    p.wait("sync", p.cur("vector"))
    p.wait("sync", p.cur("scalar"))
    ini = []
    for (dst, src) in [(wst[:], c.w1_0[:]), (b1st[:], c.b1_0[:]), (gam[:], c.gam_in[:]),
                       (bet[:], c.bet_in[:]), (mket[:], c.mketa[:]), (ones128[:], c.onesbd[:]),
                       (bdmr[:], c.bdmask_rep[:]), (tokc[:, 3:4], c.lti_c[:]),
                       (cols[:, 15:16], c.lti15[:]), (cols[:, 14:15], c.invi_c[:])]:
        ini.append(p.emit("sync", lambda e, d=dst, s=src: e.dma_start(out=d, in_=s), dma=True))
    p.wait("vector", c.P1_PJ_LAST)
    p.wait("vector", ini[-1])
    p.v(lambda e: e.memset(bdk[:], 0.0))
    p.v(lambda e: e.memset(bdq[:], 0.0))
    p.v(lambda e: e.tensor_scalar(tokc[:, 0:1], cols[:, 14:15], tokc[:, 3:4], 0.0, ALU.add, ALU.max))
    p.v(lambda e: e.tensor_single_scalar(tokc[:, 1:2], tokc[:, 0:1], -1.0, ALU.mult))
    p.v(lambda e: e.tensor_scalar(tokc[:, 2:3], cols[:, 15:16], 1.0 / 16, 0.0, ALU.add, ALU.max))
    p.v(lambda e: e.tensor_single_scalar(tokc[:, 2:3], tokc[:, 2:3], -1.0, ALU.mult))
    p.v(lambda e: e.tensor_mul(gam2[:], gam[:], gam[:]))
    p.v(lambda e: e.tensor_mul(gbt[:], gam[:], bet[:]))
    st_tok = p.v(lambda e: e.tensor_copy(wstb[:], wst[:]))
    act_prev = None
    vec_z1_prev = None

    p.wait("sync", c.P1_TM_LAST)
    p.wait("sync", c.P1_FM_LAST)
    p.wait("sync", c.P1_SG_LAST)

    Z1 = c.pb[0][:, 0:128]
    XQW = c.pb[0][:, 128:256]
    GRAM = c.pb[0][:, 256:320]
    SIG = c.pb[0][:, 320:322]
    Z1BT = c.pb[1][:, 0:128]
    B1D = c.pb[1][:, 128:256]
    W1D = c.pb[2][:, 0:512]

    ys_store = []
    chunk_last_mm = []
    chunk_last_v = []
    chunk_last_g = []
    ys_prev = None
    for chk in range(NCHK):
        t0 = chk * 256
        buf = chk % 2
        if chk >= 2:
            p.wait("sync", chunk_last_mm[chk - 2])
            p.wait("sync", chunk_last_v[chk - 2])
            p.wait("sync", chunk_last_g[chk - 2])
            p.wait("vector", ys_store[(chk - 2) * 16 + 15])
        lds = []
        for (gp, dstD) in [(fmk, c.kfmD), (fmq, c.qfmD)]:
            for pr8 in range(8):
                lds.append(p.emit("sync", lambda e, d=gp[:, (buf * 8 + pr8) * 256:(buf * 8 + pr8 + 1) * 256],
                                  s=dstD[pr8 * 128:(pr8 + 1) * 128, t0:t0 + 256]:
                                  e.dma_start(out=d, in_=s), dma=True))
        for (tmb, dD) in [(tmq, c.qtmD), (tmk, c.ktmD), (tmv, c.vtmD)]:
            for g in range(2):
                for cc in range(8):
                    src = dD.rearrange("(nn s) (hh f) -> nn s hh f", s=16, f=HD)
                    src = src[16 * chk:16 * chk + 16, :, 8 * g + cc, :].transpose([1, 0, 2])
                    dst = tmb[16 * cc:16 * cc + 16,
                              buf * 2048 + g * 1024: buf * 2048 + (g + 1) * 1024]
                    lds.append(p.emit("sync", lambda e, d=dst, s=src:
                                      e.dma_start(out=d, in_=s.opt()), dma=True))
        lds.append(p.emit("sync", lambda e, d=sigrow[0:1, buf * 4096:(buf + 1) * 4096],
                          s=c.siglrD[chk * 16:(chk + 1) * 16, :]:
                          e.dma_start(out=d, in_=s.rearrange("n cc -> (n cc)").unsqueeze(0)), dma=True))
        LDC = lds[-1]
        p.wait("vector", LDC)
        stv = None
        for (bd, fm) in [(bdk, fmk), (bdq, fmq)]:
            for pr8 in range(8):
                srcs = fm[:, (buf * 8 + pr8) * 256:(buf * 8 + pr8 + 1) * 256].rearrange(
                    "p (n ss) -> p n ss", ss=16)
                dsts = bd[:, (buf * 8 + pr8) * 512:(buf * 8 + pr8 + 1) * 512].rearrange(
                    "p (n w) -> p n w", w=32)
                stv = p.emit("vector", lambda e, d=dsts, s=srcs:
                             e.tensor_copy(d[0:64, :, 0:16], s[0:64, :, :]))
                stv = p.emit("vector", lambda e, d=dsts, s=srcs:
                             e.tensor_copy(d[64:128, :, 16:32], s[64:128, :, :]))
        BDKQ = stv
        p.wait("gpsimd", LDC)
        gx = None
        for g in range(2):
            dst = bdxk[:, (buf * 2 + g) * 2048:(buf * 2 + g + 1) * 2048].rearrange(
                "p (n w) -> p n w", w=128)
            src = tmk[:, buf * 2048 + g * 1024: buf * 2048 + (g + 1) * 1024].rearrange(
                "p (n f) -> p n f", f=64)
            gx = p.emit("gpsimd", lambda e, d=dst, s=src: e.tensor_copy(d[:, :, 0:64], s))
            p.wait("gpsimd", gx)
            gx = p.emit("gpsimd", lambda e, d=dst, s=src: e.tensor_copy(d[:, :, 64:128], s))
            p.wait("gpsimd", gx)
            gx = p.emit("gpsimd", lambda e, d=bdxk[:, (buf * 2 + g) * 2048:(buf * 2 + g + 1) * 2048]:
                        e.tensor_tensor(d, d, bdmr[:], ALU.mult))
            p.wait("gpsimd", gx)
        BDXK = gx

        for n in range(16):
            sl128 = slice(buf * 2048 + n * 128, buf * 2048 + (n + 1) * 128)
            p.wait("tensor", st_tok)
            p.wait("tensor", BDKQ)
            if act_prev is not None:
                p.wait("tensor", act_prev)
            if vec_z1_prev is not None:
                p.wait("tensor", vec_z1_prev)
            mt = None
            for g in range(2):
                for pr in range(4):
                    gp8 = g * 4 + pr
                    lz = bdk[:, (buf * 8 + gp8) * 512 + n * 32:(buf * 8 + gp8) * 512 + (n + 1) * 32]
                    lq = bdq[:, (buf * 8 + gp8) * 512 + n * 32:(buf * 8 + gp8) * 512 + (n + 1) * 32]
                    rw = wstb[:, gp8 * 64:(gp8 + 1) * 64]
                    tp = (0, 32 * pr) if pr == 3 else None
                    mt = p.emit("tensor", lambda e, lz=lz, rw=rw, g=g, pr=pr, tp=tp:
                                e.matmul(Z1[32 * pr:32 * pr + 32, 64 * g:64 * g + 64], lz, rw,
                                         start=True, stop=True, tile_position=tp), mark=False)
                    mt = p.emit("tensor", lambda e, lq=lq, rw=rw, g=g, pr=pr, tp=tp:
                                e.matmul(XQW[32 * pr:32 * pr + 32, 64 * g:64 * g + 64], lq, rw,
                                         start=True, stop=True, tile_position=tp), mark=False)
                    mt = p.emit("tensor", lambda e, lz=lz, lq=lq, g=g, pr=pr, tp=tp:
                                e.matmul(GRAM[32 * pr:32 * pr + 32, 32 * g:32 * g + 32], lz, lq,
                                         start=True, stop=True, tile_position=tp), mark=False)
            for g in range(2):
                sr = sigrow[0:1, buf * 4096 + n * 256 + g * 128: buf * 4096 + n * 256 + (g + 1) * 128]
                mt = p.emit("tensor", lambda e, sr=sr, g=g:
                            e.matmul(SIG[:, g:g + 1], sr, c.ones_sb[0:1, 0:1],
                                     start=True, stop=True), mark=(g == 1))
            m1tok = mt
            p.wait("vector", m1tok)
            p.v(lambda e: e.tensor_add(sc2[:], Z1, b1st[:]))
            sc2v = sc2[:].rearrange("p (h f) -> p h f", h=2)
            p.v(lambda e: e.reduce_sum(cols[:, 0:2], sc2v, axis=AX.X))
            p.v(lambda e: e.tensor_single_scalar(cols[:, 2:4], cols[:, 0:2], -1.0 / HD, ALU.mult))
            p.wait("scalar", p.cur("vector"))
            a1 = p.emit("scalar", lambda e: e.add(ue[:, 0:64], sc2[:, 0:64], cols[:, 2:3]))
            a1 = p.emit("scalar", lambda e: e.add(ue[:, 64:128], sc2[:, 64:128], cols[:, 3:4]))
            p.wait("scalar", a1)
            a2 = p.emit("scalar", lambda e: e.activation(gxh[:, 0:64], ue[:, 0:64], AF.Square,
                                                         accum_out=cols[:, 4:5]))
            a2 = p.emit("scalar", lambda e: e.activation(gxh[:, 64:128], ue[:, 64:128], AF.Square,
                                                         accum_out=cols[:, 5:6]))
            act_tok = a2
            p.wait("gpsimd", BDXK)
            if ys_prev is not None:
                p.wait("gpsimd", ys_prev)
            tv0 = tmv[:, buf * 2048 + n * 64: buf * 2048 + (n + 1) * 64]
            tv1 = tmv[:, buf * 2048 + 1024 + n * 64: buf * 2048 + 1024 + (n + 1) * 64]
            tk0 = tmk[:, buf * 2048 + n * 64: buf * 2048 + (n + 1) * 64]
            tk1 = tmk[:, buf * 2048 + 1024 + n * 64: buf * 2048 + 1024 + (n + 1) * 64]
            g1 = p.emit("gpsimd", lambda e, a=tv0, b=tk0:
                        e.tensor_tensor(sc1[:, 0:64], a, b, ALU.subtract))
            g1 = p.emit("gpsimd", lambda e, a=tv1, b=tk1:
                        e.tensor_tensor(sc1[:, 64:128], a, b, ALU.subtract))
            p.wait("gpsimd", g1)
            g1 = p.emit("gpsimd", lambda e: e.tensor_tensor(sc1[:], sc1[:], gam[:], ALU.mult))
            p.wait("gpsimd", g1)
            g1 = p.emit("gpsimd", lambda e: e.tensor_tensor(ctt[:], gbt[:], sc1[:], ALU.subtract))
            ct_tok = g1
            p.wait("vector", act_tok)
            p.v(lambda e: e.tensor_scalar(cols[:, 6:8], cols[:, 4:6], 1.0 / HD, EPST,
                                          ALU.mult, ALU.add))
            emit_rsqrt(p, cols[:, 6:8], cols[:, 8:10], cols[:, 10:12], iters=2)
            p.v(lambda e: e.tensor_scalar(xh[:, 0:64], ue[:, 0:64], cols[:, 8:9], None, ALU.mult))
            xh_tok = p.v(lambda e: e.tensor_scalar(xh[:, 64:128], ue[:, 64:128], cols[:, 9:10],
                                                   None, ALU.mult))
            p.wait("gpsimd", xh_tok)
            g2 = p.emit("gpsimd", lambda e: e.tensor_tensor(gxh[:], gam2[:], xh[:], ALU.mult))
            p.wait("gpsimd", g2)
            p.wait("gpsimd", ct_tok)
            g2 = p.emit("gpsimd", lambda e: e.tensor_tensor(gxh[:], gxh[:], ctt[:], ALU.add))
            gxh_tok = g2
            p.wait("vector", gxh_tok)
            gxhv = gxh[:].rearrange("p (h f) -> p h f", h=2)
            p.v(lambda e: e.reduce_sum(cols[:, 0:2], gxhv, axis=AX.X))
            p.v(lambda e: e.tensor_mul(sc2[:], gxh[:], xh[:]))
            sc2v = sc2[:].rearrange("p (h f) -> p h f", h=2)
            p.v(lambda e: e.reduce_sum(cols[:, 2:4], sc2v, axis=AX.X))
            p.v(lambda e: e.tensor_single_scalar(cols[:, 0:4], cols[:, 0:4], 1.0 / HD, ALU.mult))
            p.v(lambda e: e.tensor_scalar(sc1[:, 0:64], xh[:, 0:64], cols[:, 2:3], cols[:, 0:1],
                                          ALU.mult, ALU.add))
            p.v(lambda e: e.tensor_scalar(sc1[:, 64:128], xh[:, 64:128], cols[:, 3:4], cols[:, 1:2],
                                          ALU.mult, ALU.add))
            p.v(lambda e: e.tensor_sub(sc1[:], gxh[:], sc1[:]))
            p.v(lambda e: e.scalar_tensor_tensor(cols[:, 4:6], cols[:, 8:10], 1.0 / HD, SIG,
                                                 ALU.mult, ALU.mult))
            p.v(lambda e: e.tensor_scalar(grad[:, 0:64], sc1[:, 0:64], cols[:, 4:5], None, ALU.mult))
            gr_tok = p.v(lambda e: e.tensor_scalar(grad[:, 64:128], sc1[:, 64:128], cols[:, 5:6],
                                                   None, ALU.mult))
            mt_tok = p.v(lambda e: e.scalar_tensor_tensor(mtt[:], GRAM, 1.0, mket[:],
                                                          ALU.add, ALU.mult))
            vec_z1_prev = mt_tok
            p.wait("tensor", gr_tok)
            p.wait("tensor", mt_tok)
            m2 = None
            for g in range(2):
                for pr in range(4):
                    gp8 = g * 4 + pr
                    tp = (32 * pr, 32 * pr)
                    gsl = grad[32 * pr:32 * pr + 32, 64 * g:64 * g + 64]
                    m2 = p.emit("tensor", lambda e, g=g, pr=pr, gsl=gsl, tp=tp:
                                e.matmul(Z1BT[32 * pr:32 * pr + 32, 64 * g:64 * g + 64],
                                         mtt[32 * pr:32 * pr + 32, 32 * g:32 * g + 32], gsl,
                                         start=True, stop=True, tile_position=tp), mark=False)
                    m2 = p.emit("tensor", lambda e, g=g, pr=pr, gsl=gsl, tp=tp:
                                e.matmul(B1D[32 * pr:32 * pr + 32, 64 * g:64 * g + 64],
                                         ones128[32 * pr:32 * pr + 32, :], gsl,
                                         start=True, stop=True, tile_position=tp), mark=False)
                    bxsl = bdxk[32 * pr:32 * pr + 32,
                                (buf * 2 + g) * 2048 + n * 128:(buf * 2 + g) * 2048 + (n + 1) * 128]
                    m2 = p.emit("tensor", lambda e, g=g, pr=pr, gsl=gsl, bxsl=bxsl, tp=(32 * pr, 0):
                                e.matmul(W1D[:, gp8 * 64:(gp8 + 1) * 64] if False else
                                         c.pb[2][:, (g * 4 + pr) * 64:(g * 4 + pr + 1) * 64],
                                         bxsl, gsl, start=True, stop=True, tile_position=tp),
                                mark=(g == 1 and pr == 3))
            m2tok = m2
            p.wait("vector", m2tok)
            p.v(lambda e: e.tensor_add(sc2[:], XQW, b1st[:]))
            p.v(lambda e: e.scalar_tensor_tensor(zbar[:], Z1BT, tokc[:, 1:2], sc2[:],
                                                 ALU.mult, ALU.add))
            p.v(lambda e: e.scalar_tensor_tensor(wst[:, 0:256], c.pb[2][:, 0:256], tokc[:, 2:3],
                                                 wst[:, 0:256], ALU.mult, ALU.add))
            p.v(lambda e: e.scalar_tensor_tensor(wst[:, 256:512], c.pb[2][:, 256:512], tokc[:, 2:3],
                                                 wst[:, 256:512], ALU.mult, ALU.add))
            st_tok = p.v(lambda e: e.tensor_copy(wstb[:], wst[:]))
            p.v(lambda e: e.scalar_tensor_tensor(b1st[:], B1D, tokc[:, 2:3], b1st[:],
                                                 ALU.mult, ALU.add))
            zv = zbar[:].rearrange("p (h f) -> p h f", h=2)
            p.v(lambda e: e.reduce_sum(cols[:, 0:2], zv, axis=AX.X))
            p.v(lambda e: e.tensor_single_scalar(cols[:, 2:4], cols[:, 0:2], -1.0 / HD, ALU.mult))
            p.wait("scalar", p.cur("vector"))
            a3 = p.emit("scalar", lambda e: e.add(ue[:, 0:64], zbar[:, 0:64], cols[:, 2:3]))
            a3 = p.emit("scalar", lambda e: e.add(ue[:, 64:128], zbar[:, 64:128], cols[:, 3:4]))
            p.wait("scalar", a3)
            a4 = p.emit("scalar", lambda e: e.activation(sc2[:, 0:64], ue[:, 0:64], AF.Square,
                                                         accum_out=cols[:, 4:5]))
            a4 = p.emit("scalar", lambda e: e.activation(sc2[:, 64:128], ue[:, 64:128], AF.Square,
                                                         accum_out=cols[:, 5:6]))
            act_prev = a4
            p.wait("vector", a4)
            p.v(lambda e: e.tensor_scalar(cols[:, 6:8], cols[:, 4:6], 1.0 / HD, EPST,
                                          ALU.mult, ALU.add))
            emit_rsqrt(p, cols[:, 6:8], cols[:, 8:10], cols[:, 10:12], iters=2)
            p.v(lambda e: e.tensor_scalar(sc1[:, 0:64], ue[:, 0:64], cols[:, 8:9], None, ALU.mult))
            p.v(lambda e: e.tensor_scalar(sc1[:, 64:128], ue[:, 64:128], cols[:, 9:10], None, ALU.mult))
            p.v(lambda e: e.tensor_mul(sc1[:], sc1[:], gam[:]))
            p.v(lambda e: e.tensor_add(sc1[:], sc1[:], bet[:]))
            tq0 = tmq[:, buf * 2048 + n * 64: buf * 2048 + (n + 1) * 64]
            tq1 = tmq[:, buf * 2048 + 1024 + n * 64: buf * 2048 + 1024 + (n + 1) * 64]
            yd0 = yschunk[:, buf * 2048 + n * 64: buf * 2048 + (n + 1) * 64]
            yd1 = yschunk[:, buf * 2048 + 1024 + n * 64: buf * 2048 + 1024 + (n + 1) * 64]
            p.v(lambda e, a=tq0, d=yd0: e.tensor_add(d, sc1[:, 0:64], a))
            ys_tok = p.v(lambda e, a=tq1, d=yd1: e.tensor_add(d, sc1[:, 64:128], a))
            ys_prev = ys_tok
        chunk_last_mm.append(m2tok)
        chunk_last_v.append(ys_tok)
        chunk_last_g.append(gxh_tok)
        p.wait("gpsimd", ys_tok)
        for g in range(2):
            for cc in range(8):
                dstys = c.ysD.rearrange("(nn s) (hh f) -> nn s hh f", s=16, f=HD)
                dstys = dstys[16 * chk:16 * chk + 16, :, 8 * g + cc, :].transpose([1, 0, 2])
                srcys = yschunk[16 * cc:16 * cc + 16,
                                buf * 2048 + g * 1024: buf * 2048 + (g + 1) * 1024]
                ys_store.append(p.emit("gpsimd", lambda e, d=dstys, s=srcys:
                                       e.dma_start(out=d, in_=s.opt()), dma=True))
    c.YS_LAST = ys_store[-1]
    sb2.close()


def emit_collective(c):
    p = c.p
    p.wait("gpsimd", c.YS_LAST)
    c.CC_TOK = p.emit("gpsimd", lambda e: e.collective_compute(
        "AllGather", ALU.bypass, [[2 * i, 2 * i + 1] for i in range(4)],
        [c.ysD.opt()], [c.ccout.opt()]))


def emit_stub_out(c):
    p, nc = c.p, c.nc
    sb = contextlib.ExitStack()
    zz = sb.enter_context(nc.sbuf_tensor("zz", [128, D], F32))
    zc = p.v(lambda e: e.memset(zz[:], 0.0))
    p.wait("gpsimd", zc)
    last = None
    for tt in range(c.NTO):
        last = p.emit("gpsimd", lambda e, d=c.out[tt * 128:(tt + 1) * 128, :]:
                      e.dma_start(out=d, in_=zz[:]), dma=True)
    emit_final_waits(c)
    sb.close()


def emit_final_waits(c):
    p = c.p
    for key in ("dsync", "dvector", "dscalar", "dtensor", "dgpsimd",
                "sync", "vector", "scalar", "tensor"):
        p.wait("gpsimd", (key, p.sems[key][1]))
    p.wait("gpsimd", ("dgpsimd", p.sems["dgpsimd"][1]))


def build_nc(L=4096, n_exp=NE, phases=1, use_collective=True, dbg_ffn_only=False, dbg_skip=(),
             dbg_ys=False):
    c = make_ctx(L, n_exp)
    declare_io(c)
    emit_phase1(c)
    if phases >= 2:
        emit_phase2(c)
        if use_collective:
            emit_collective(c)
    if dbg_ys:
        p = c.p
        p.wait("gpsimd", c.YS_LAST)
        if dbg_ys in ("cc0", "cc1") and use_collective:
            p.wait("gpsimd", c.CC_TOK)
            roff = 0 if dbg_ys == "cc0" else L
            src = c.ccout[roff:roff + L, :].rearrange("(a b) f -> a (b f)", b=2)
        else:
            src = c.ysD.rearrange("(a b) f -> a (b f)", b=2)
        last = p.emit("gpsimd", lambda e, src=src: e.dma_start(
            out=c.out[:, :], in_=src), dma=True)
        p.wait("sync", last)
        emit_final_waits(c)
        with c.es, c.nc.Block() as block:
            c.p.run(block)
        return c.nc
    if phases >= 3:
        emit_phase3(c, use_collective, dbg_ffn_only, dbg_skip)
    else:
        emit_stub_out(c)
    with c.es, c.nc.Block() as block:
        c.p.run(block)
    return c.nc



def emit_phase3(c, use_collective=True, dbg_ffn_only=False, dbg_skip=()):
    p, nc, L, TOWN, n_exp = c.p, c.nc, c.L, c.TOWN, c.n_exp
    TT = TOWN // 2
    NTB = TT // 128
    sb3 = contextlib.ExitStack()
    sbt = lambda n, sh, dt=F32: sb3.enter_context(nc.sbuf_tensor(n, sh, dt))
    xnTh = sbt("xnTh", [128, max(16 * TT, 16384)], BF16)
    xstg = sbt("xstg", [128, 2 * TT], BF16)
    wab = sbt("wab", [128, 16 * 16], BF16)
    wstg3 = sbt("wstg3", [128, 256])
    comb = sbt("comb", [128, NTB * 16])
    combT = sbt("combT", [128, NTB * 128])
    gbuf = sbt("gbuf", [128, 2 * 16 * 256], BF16)
    w1buf = sbt("w1buf", [128, 2 * 16 * 256], BF16)
    w2buf = sbt("w2buf", [128, 2 * 8 * 512], BF16)
    Yb = sbt("Yb", [128, max(8 * TT, 8192)], BF16)
    Gb = sbt("Gb", [128, 2 * 512])
    acc = sbt("acc", [128, NTB * 2048])
    gbc = sbt("gbc", [128, (NE) * 8])
    b1c3 = sbt("b1c3", [128, (NE) * 8])
    rb2s = sbt("rb2s", [128, D])
    sb2b = sbt("sb2b", [128, D])
    colz = sbt("colz", [128, 16])
    xnThF = xnTh.ap().bitcast(F32)
    pwrep = xnThF[:, 4096:6144]
    pbrep = xnThF[:, 6144:8192]
    w1f32 = w1buf.ap().bitcast(F32)
    ysf = w1f32[:, 0:2048]
    scr3 = w1f32[:, 2048:4096]
    ysn = xnTh[:, 0:4096]
    ysnT = xnTh[:, 4096:8192]
    xrs = Yb.ap().bitcast(F32)
    owbuf0 = w2buf[:, 0:8192]
    owbuf1 = gbuf[:, 0:8192]

    p.wait("sync", c.YS_LAST)
    p.wait("sync", p.cur("vector"))
    p.wait("sync", p.cur("scalar"))
    p.wait("sync", p.cur("tensor"))
    p.wait("sync", p.cur("gpsimd"))

    ini = []
    ini.append(p.emit("sync", lambda e: e.dma_start(out=gbc[:, 0:n_exp * 8], in_=c.fgb[:]), dma=True))
    ini.append(p.emit("sync", lambda e: e.dma_start(out=b1c3[:, 0:n_exp * 8], in_=c.fb1[:]), dma=True))
    ini.append(p.emit("sync", lambda e: e.dma_start(
        out=wstg3[:].rearrange("pp (k ee) -> pp k ee", ee=16), in_=c.wa_in.rearrange("(k pp) ee -> k pp ee", pp=128).transpose([1, 0, 2])), dma=True))
    ini.append(p.emit("sync", lambda e: e.dma_start(out=rb2s[:], in_=c.frb2[:]), dma=True))
    p.wait("vector", ini[-1])
    p.v(lambda e: e.tensor_copy(wab[:], wstg3[:]))

    MV = min(512, TOWN // 2)
    out_stores = []
    evlast = {}
    ge_toks, ge_toks2, y_toks, ev_toks, w2mm_toks, wld = {}, {}, {}, {}, {}, {}
    for tmove in range(2):
        tbase = tmove * TT
        if tmove >= 1:
            p.wait("sync", p.cur("tensor"))
            p.wait("sync", p.cur("vector"))
            p.wait("sync", p.cur("scalar"))
            if out_stores:
                p.wait("sync", out_stores[-1])
        xn_tok = None
        for k in range(16):
            lc0 = p.emit("sync", lambda e, k=k, d=xstg[:, 0:TT],
                         s=c.xnT_D[k * 128:(k + 1) * 128, tbase:tbase + TT]:
                         e.dma_start(out=d, in_=s), dma=True)
            lc1 = p.emit("sync", lambda e, k=k, d=xstg[:, TT:2 * TT],
                         s=c.xnT_D[k * 128:(k + 1) * 128, TOWN + tbase:TOWN + tbase + TT]:
                         e.dma_start(out=d, in_=s), dma=True)
            p.wait("vector", lc1)
            p.v(lambda e: e.tensor_sub(xstg[:, 0:TT], xstg[:, 0:TT], xstg[:, TT:2 * TT]))
            xn_tok = p.v(lambda e, d=xnTh[:, k * TT:(k + 1) * TT]:
                         e.scalar_tensor_tensor(d, xstg[:, 0:TT], c.m0c[:, 0:1],
                                                xstg[:, TT:2 * TT], ALU.mult, ALU.add))
            p.wait("sync", xn_tok)
        p.wait("tensor", xn_tok)
        sg_toks = []
        for tb in range(NTB):
            if sg_toks:
                p.wait("tensor", sg_toks[-1])
            mmt = None
            for k in range(16):
                mmt = p.emit("tensor", lambda e, k=k, tb=tb:
                             e.matmul(c.pb[7][0:128, 0:16], xnTh[:, k * TT + tb * 128:k * TT + (tb + 1) * 128],
                                      wab[:, k * 16:(k + 1) * 16], start=(k == 0), stop=(k == 15)),
                             mark=(k == 15))
            p.wait("scalar", mmt)
            if tb >= 1:
                p.wait("scalar", v_comb)
            sg = p.emit("scalar", lambda e, tb=tb: e.activation(wstg3[:, 0:16], c.pb[7][0:128, 0:16],
                                                                AF.Sigmoid))
            sg_toks.append(sg)
            p.wait("vector", sg)
            p.v(lambda e: e.reduce_max(colz[:, 0:1], wstg3[:, 0:16], axis=AX.X))
            p.v(lambda e: e.tensor_scalar(wstg3[:, 16:32], wstg3[:, 0:16], colz[:, 0:1], None, ALU.is_equal))
            p.v(lambda e: e.scalar_tensor_tensor(wstg3[:, 16:32], wstg3[:, 16:32], -1e30, wstg3[:, 0:16],
                                                 ALU.mult, ALU.add))
            p.v(lambda e: e.reduce_max(colz[:, 1:2], wstg3[:, 16:32], axis=AX.X))
            p.v(lambda e: e.tensor_scalar(colz[:, 2:3], colz[:, 0:1], colz[:, 1:2], None, ALU.add))
            p.v(lambda e: e.reciprocal(colz[:, 3:4], colz[:, 2:3]))
            p.v(lambda e: e.tensor_scalar(wstg3[:, 16:32], wstg3[:, 0:16], colz[:, 1:2], None, ALU.is_ge))
            p.v(lambda e: e.tensor_mul(wstg3[:, 16:32], wstg3[:, 16:32], wstg3[:, 0:16]))
            v_comb = p.v(lambda e, tb=tb: e.tensor_scalar(comb[:, tb * 16:(tb + 1) * 16],
                                                          wstg3[:, 16:32], colz[:, 3:4], None, ALU.mult))
            if "rcomb" in dbg_skip:
                v_comb = p.v(lambda e, tb=tb: e.memset(comb[:, tb * 16:(tb + 1) * 16], 0.0625))
            if "combT" not in dbg_skip:
                if tb == 0:
                    pz = p.v(lambda e: e.memset(acc[:, 0:128], 0.0))
                cpad = p.v(lambda e, tb=tb: e.tensor_copy(acc[:, 0:16],
                                                          comb[:, tb * 16:(tb + 1) * 16]))
                p.wait("tensor", cpad)
                if tb >= 1 or tmove >= 1:
                    p.wait("tensor", ctv)
                trt = p.emit("tensor", lambda e: e.transpose(
                    c.pb[6][:, 0:128], acc[:, 0:128], c.idf[:]))
                p.wait("vector", trt)
                ctv = p.v(lambda e, tb=tb: e.tensor_copy(combT[:, tb * 128:(tb + 1) * 128],
                                                         c.pb[6][:, 0:128]))
        if "snapcomb" in dbg_skip:
            p.v(lambda e: e.tensor_copy(wstg3[:, 64:64 + NTB * 16], comb[:, :]))
        ROUTED = p.cur("vector")

        def emit_wload(ei, jp):
            par = jp % 2
            for (buf_, wb_) in [(gbuf, c.wbG), (w1buf, c.wbW1)]:
                src = wb_.rearrange("(ek pp) f -> pp ek f", pp=128)[:, ei * 16:ei * 16 + 16,
                                                                   jp * 256:(jp + 1) * 256]
                key = (tmove, ei, jp, id(buf_))
                prev = None
                if jp >= 2:
                    prev = ge_toks.get((tmove, ei, jp - 2))
                elif (tmove, ei - 1, jp + 2) in ge_toks:
                    prev = ge_toks.get((tmove, ei - 1, jp + 2))
                elif tmove == 1 and ei == 0:
                    prev = ge_toks.get((0, n_exp - 1, jp + 2))
                if prev is not None:
                    p.wait("sync", prev)
                wld[key] = p.emit(
                    "sync", lambda e, d=buf_[:, par * 4096:(par + 1) * 4096], s=src:
                    e.dma_start(out=d.rearrange("p (k f) -> p k f", k=16), in_=s.opt()), dma=True)

        for ei in range(n_exp):
            if ei == 0:
                emit_wload(ei, 0)
                emit_wload(ei, 1)
            for j in range(8):
                jp = j // 2
                par = jp % 2
                jj = j % 2
                gp_ = c.pb[j % 2]
                hp_ = c.pb[2 + j % 2]
                p.wait("tensor", wld[(tmove, ei, jp, id(w1buf))])
                p.wait("tensor", wld[(tmove, ei, jp, id(gbuf))])
                if ei == 0 and j == 0:
                    p.wait("tensor", ROUTED)
                for tmv2 in range(TT // MV):
                    MVL = TT // MV - 1
                    if tmv2 >= 1:
                        kprev = (tmove, ei, j, tmv2 - 1)
                    elif j >= 2:
                        kprev = (tmove, ei, j - 2, MVL)
                    elif ei >= 1:
                        kprev = (tmove, ei - 1, 6 + j, MVL)
                    elif tmove == 1:
                        kprev = (0, n_exp - 1, 6 + j, MVL)
                    else:
                        kprev = None
                    if kprev is not None and kprev in y_toks:
                        p.wait("tensor", y_toks[kprev])
                        p.wait("tensor", ge_toks2[kprev])
                    mmg = mmh = None
                    for k in range(16):
                        lh = gbuf[:, par * 4096 + k * 256 + jj * 128: par * 4096 + k * 256 + (jj + 1) * 128]
                        rh = xnTh[:, k * TT + tmv2 * MV: k * TT + (tmv2 + 1) * MV]
                        mmg = p.emit("tensor", lambda e, k=k, lh=lh, rh=rh, d=gp_[:, 0:MV]:
                                     e.matmul(d, lh, rh, start=(k == 0), stop=(k == 15)),
                                     mark=(k == 15))
                    for k in range(16):
                        lh = w1buf[:, par * 4096 + k * 256 + jj * 128: par * 4096 + k * 256 + (jj + 1) * 128]
                        rh = xnTh[:, k * TT + tmv2 * MV: k * TT + (tmv2 + 1) * MV]
                        mmh = p.emit("tensor", lambda e, k=k, lh=lh, rh=rh, d=hp_[:, 0:MV]:
                                     e.matmul(d, lh, rh, start=(k == 0), stop=(k == 15)),
                                     mark=(k == 15))
                    ge_toks[(tmove, ei, jp)] = mmh
                    p.wait("scalar", mmg)
                    if kprev is not None and kprev in y_toks:
                        p.wait("scalar", y_toks[kprev])
                    gel = p.emit("scalar", lambda e, d=Gb[:, (j % 2) * 512:(j % 2) * 512 + MV],
                                 s=gp_[:, 0:MV], b=gbc[:, ei * 8 + j:ei * 8 + j + 1]:
                                 e.activation(d, s, AF.Gelu, bias=b))
                    ge_toks2[(tmove, ei, j, tmv2)] = gel
                    p.wait("vector", gel)
                    p.wait("vector", mmh)
                    y_toks[(tmove, ei, j, tmv2)] = p.v(
                        lambda e, d=Yb[:, j * TT + tmv2 * MV: j * TT + (tmv2 + 1) * MV],
                        s=hp_[:, 0:MV], b=b1c3[:, ei * 8 + j:ei * 8 + j + 1],
                        g2=Gb[:, (j % 2) * 512:(j % 2) * 512 + MV]:
                        e.scalar_tensor_tensor(d, s, b, g2, ALU.add, ALU.mult))
                if j % 2 == 1:
                    nj = jp + 2
                    if nj < 4:
                        emit_wload(ei, nj)
                    elif ei + 1 < n_exp:
                        emit_wload(ei + 1, nj - 4)
            for dm in range(4):
                src = c.wbW2.rearrange("(ej pp) dd -> pp ej dd", pp=128)[:, ei * 8:ei * 8 + 8,
                                                                        dm * 512:(dm + 1) * 512]
                par2 = dm % 2
                prev = w2mm_toks.get((tmove, ei, dm - 2)) or (
                    w2mm_toks.get((tmove, ei - 1, dm + 2)) if dm < 2 else None) or (
                    w2mm_toks.get((0, n_exp - 1, dm + 2)) if tmove == 1 and ei == 0 and dm < 2 else None)
                if prev is not None:
                    p.wait("sync", prev)
                w2ld = p.emit("sync", lambda e, d=w2buf[:, par2 * 4096:(par2 + 1) * 4096], s=src:
                              e.dma_start(out=d.rearrange("p (j dd) -> p j dd", j=8), in_=s.opt()),
                              dma=True)
                p.wait("tensor", w2ld)
                p.wait("tensor", y_toks[(tmove, ei, 7, TT // MV - 1)])
                for tb in range(NTB):
                    wp_ = c.pb[4 + tb % 2]
                    if evlast.get(tb % 2) is not None:
                        p.wait("tensor", evlast[tb % 2])
                    mmw = None
                    for j in range(8):
                        lh = Yb[:, j * TT + tb * 128: j * TT + (tb + 1) * 128]
                        rh = w2buf[:, par2 * 4096 + j * 512: par2 * 4096 + (j + 1) * 512]
                        mmw = p.emit("tensor", lambda e, j=j, lh=lh, rh=rh, d=wp_[:, :]:
                                     e.matmul(d, lh, rh, start=(j == 0), stop=(j == 7)),
                                     mark=(j == 7))
                    w2mm_toks[(tmove, ei, dm)] = mmw
                    p.wait("vector", mmw)
                    asl = acc[:, tb * 2048 + dm * 512: tb * 2048 + (dm + 1) * 512]
                    if ei < NRE:
                        csl = comb[:, tb * 16 + ei: tb * 16 + ei + 1]
                        if ei == 0:
                            ev_toks[(tmove, ei, dm, tb)] = p.v(
                                lambda e, d=asl, s=wp_[:, :], cs=csl:
                                e.tensor_scalar(d, s, cs, None, ALU.mult))
                            evlast[tb % 2] = ev_toks[(tmove, ei, dm, tb)]
                        else:
                            ev_toks[(tmove, ei, dm, tb)] = p.v(
                                lambda e, d=asl, s=wp_[:, :], cs=csl:
                                e.scalar_tensor_tensor(d, s, cs, d, ALU.mult, ALU.add))
                        evlast[tb % 2] = ev_toks[(tmove, ei, dm, tb)]
                    else:
                        ev_toks[(tmove, ei, dm, tb)] = p.v(
                            lambda e, d=asl, s=wp_[:, :]: e.tensor_add(d, s, d))
                    evlast[tb % 2] = ev_toks[(tmove, ei, dm, tb)]
        EXP_DONE = p.cur("vector")

        if dbg_ffn_only and "dumpcomb2" in dbg_skip:
            dv = p.cur("vector")
            p.wait("gpsimd", dv)
            if tmove == 0:
                for tb in range(NTB):
                    out_stores.append(p.emit("gpsimd", lambda e, tb=tb,
                                             s=acc[:, tb * 2048:(tb + 1) * 2048]:
                                             e.dma_start(out=c.out[tb * 128:(tb + 1) * 128, :], in_=s),
                                             dma=True))
                out_stores.append(p.emit("gpsimd", lambda e:
                                         e.dma_start(out=c.out[TT:TT + 128, 0:NTB * 16],
                                                     in_=comb[:, :]), dma=True))
            p.wait("sync", out_stores[-1])
            continue
        if dbg_ffn_only and "dumpcomb" in dbg_skip:
            dv = p.cur("vector")
            p.wait("gpsimd", dv)
            out_stores.append(p.emit("gpsimd", lambda e, tmove=tmove:
                                     e.dma_start(out=c.out[tmove * 128:(tmove + 1) * 128, 0:NTB * 16],
                                                 in_=comb[:, :]), dma=True))
            out_stores.append(p.emit("gpsimd", lambda e, tmove=tmove:
                                     e.dma_start(out=c.out[tmove * 128:(tmove + 1) * 128,
                                                           1024:1024 + 256],
                                                 in_=wstg3[:, 0:256]), dma=True))
            p.wait("sync", out_stores[-1])
            continue
        if dbg_ffn_only:
            dv = p.cur("vector")
            p.wait("gpsimd", dv)
            for tb in range(NTB):
                out_stores.append(p.emit("gpsimd", lambda e, tmove=tmove, tb=tb,
                                         s=acc[:, tb * 2048:(tb + 1) * 2048]:
                                         e.dma_start(out=c.out[tmove * TT + tb * 128:
                                                               tmove * TT + (tb + 1) * 128, :], in_=s),
                                         dma=True))
            p.wait("sync", out_stores[-1])
            continue
        rbev = {}
        for tb in (range(NTB) if "rb2" not in dbg_skip else ()):
            for dm in range(4):
                bk = 6 + (dm % 2)
                if (tb, dm - 2) in rbev:
                    p.wait("tensor", rbev[(tb, dm - 2)])
                elif (tb - 1, dm + 2) in rbev:
                    p.wait("tensor", rbev[(tb - 1, dm + 2)])
                mmt = p.emit("tensor", lambda e, tb=tb, dm=dm, bk=bk:
                             e.matmul(c.pb[bk][:, :], combT[:, tb * 128:(tb + 1) * 128],
                                      rb2s[:, dm * 512:(dm + 1) * 512], start=True, stop=True))
                p.wait("vector", mmt)
                rbev[(tb, dm)] = p.v(lambda e, d=acc[:, tb * 2048 + dm * 512: tb * 2048 + (dm + 1) * 512],
                                     s=c.pb[bk][:, :], bk=bk: e.tensor_add(d, s, d))

        if use_collective:
            p.wait("sync", c.CC_TOK)
        p.wait("tensor", p.cur("vector"))
        p.wait("sync", p.cur("tensor"))
        rl = []
        rl.append(p.emit("sync", lambda e: e.dma_start(out=scr3[0:1, 0:D], in_=c.postw_in[:]), dma=True))
        rl.append(p.emit("sync", lambda e: e.dma_start(out=ysf[0:1, 0:D], in_=c.postb_in[:]), dma=True))
        rl.append(p.emit("sync", lambda e: e.dma_start(out=sb2b[0:1, 0:D], in_=c.fsb2a[:]), dma=True))
        rl.append(p.emit("sync", lambda e: e.dma_start(out=pbrep[0:1, 0:D], in_=c.fsb2b[:]), dma=True))
        p.wait("vector", rl[-1])
        sb2tok = p.v(lambda e: e.tensor_add(sb2b[0:1, 0:D], sb2b[0:1, 0:D], pbrep[0:1, 0:D]))
        p.wait("tensor", sb2tok)
        for (dst, srow) in [(pwrep, scr3[0:1, 0:D]), (pbrep, ysf[0:1, 0:D]),
                            (None, sb2b[0:1, 0:D])]:
            for dm in range(4):
                mmt = p.emit("tensor", lambda e, d=c.pb[6][:, :], s=srow[0:1, dm * 512:(dm + 1) * 512]:
                             e.matmul(d, c.ones_sb[0:1, :], s, start=True, stop=True))
                p.wait("vector", mmt)
                dd = (dst if dst is not None else sb2b)
                bce = p.v(lambda e, d=dd[:, dm * 512:(dm + 1) * 512], s=c.pb[6][:, :]:
                          e.tensor_copy(d, s))
                p.wait("tensor", bce)
        TIP = min(2, NTB)
        ow_prev = None
        res_fin = None
        for tbp in range(NTB // TIP):
            ysT_toks = []
            for ti in range(TIP):
                tb = tbp * TIP + ti
                tglob = tbase + tb * 128
                if ti == 0 and res_fin is not None:
                    p.wait("sync", res_fin)
                for (rk, coff) in [(0, 0), (1, 1024)]:
                    rbase = rk * L
                    l0 = p.emit("sync", lambda e, d=xrs[:, 0:1024],
                                s=c.ccout[rbase + tglob: rbase + tglob + 128, :]:
                                e.dma_start(out=d, in_=s), dma=True)
                    l1 = p.emit("sync", lambda e, d=xrs[:, 1024:2048],
                                s=c.ccout[rbase + TOWN + tglob: rbase + TOWN + tglob + 128, :]:
                                e.dma_start(out=d, in_=s), dma=True)
                    p.wait("vector", l1)
                    p.v(lambda e: e.tensor_sub(xrs[:, 0:1024], xrs[:, 0:1024], xrs[:, 1024:2048]))
                    bl = p.v(lambda e, d=ysf[:, coff:coff + 1024]:
                             e.scalar_tensor_tensor(d, xrs[:, 0:1024], c.m0c[:, 0:1],
                                                    xrs[:, 1024:2048], ALU.mult, ALU.add))
                    p.wait("sync", bl)
                p.v(lambda e: e.reduce_sum(colz[:, 4:5], ysf[:], axis=AX.X))
                p.v(lambda e: e.tensor_single_scalar(colz[:, 5:6], colz[:, 4:5], -1.0 / D, ALU.mult))
                p.wait("scalar", p.cur("vector"))
                ua = p.emit("scalar", lambda e: e.add(scr3[:], ysf[:], colz[:, 5:6]))
                p.wait("scalar", ua)
                sqa = p.emit("scalar", lambda e: e.activation(ysf[:], scr3[:], AF.Square,
                                                              accum_out=colz[:, 6:7]))
                p.wait("vector", sqa)
                p.v(lambda e: e.tensor_scalar(colz[:, 7:8], colz[:, 6:7], 1.0 / D, EPST,
                                              ALU.mult, ALU.add))
                emit_rsqrt(p, colz[:, 7:8], colz[:, 8:9], colz[:, 9:10])
                p.v(lambda e: e.tensor_scalar(scr3[:], scr3[:], colz[:, 8:9], None, ALU.mult))
                p.v(lambda e: e.tensor_mul(scr3[:], scr3[:], pwrep[:]))
                nrm = p.v(lambda e, d=ysn[:, ti * 2048:(ti + 1) * 2048]:
                          e.tensor_add(d, scr3[:], pbrep[:]))
                p.wait("tensor", nrm)
                for k in range(16):
                    tsl = 2 + (k % 2)
                    trp = p.emit("tensor", lambda e, k=k, ti=ti, tsl=tsl:
                                 e.transpose(c.pbB[tsl][:, 0:128],
                                             ysn[:, ti * 2048 + k * 128: ti * 2048 + (k + 1) * 128],
                                             c.idb[:]))
                    p.wait("vector", trp)
                    tv = p.v(lambda e, k=k, ti=ti, tsl=tsl:
                             e.tensor_copy(ysnT[:, (ti * 16 + k) * 128:(ti * 16 + k + 1) * 128],
                                           c.pbB[tsl][:, 0:128]))
                    p.wait("tensor", tv)
                ysT_toks.append(p.cur("vector"))
            p.wait("tensor", ysT_toks[-1])
            mmo = None
            ow_dm = []
            for dm in range(4):
                ob = owbuf0 if dm % 2 == 0 else owbuf1
                if dm >= 2:
                    p.wait("sync", ow_dm[dm - 2])
                elif ow_prev is not None:
                    p.wait("sync", ow_prev[dm + 2])
                src = c.wbO.rearrange("(k pp) dd -> pp k dd", pp=128)[:, :, dm * 512:(dm + 1) * 512]
                owl = p.emit("sync", lambda e, d=ob, s=src:
                             e.dma_start(out=d.rearrange("p (k dd) -> p k dd", k=16), in_=s.opt()),
                             dma=True)
                p.wait("tensor", owl)
                for ti in range(TIP):
                    for k in range(16):
                        mmo = p.emit("tensor", lambda e, k=k, ti=ti, dm=dm, ob=ob:
                                     e.matmul(c.pb[ti * 4 + dm][:, :],
                                              ysnT[:, (ti * 16 + k) * 128:(ti * 16 + k + 1) * 128],
                                              ob[:, k * 512:(k + 1) * 512],
                                              start=(k == 0), stop=(k == 15)),
                                     mark=(k == 15))
                ow_dm.append(mmo)
            ow_prev = list(ow_dm)
            p.wait("vector", mmo)
            for ti in range(TIP):
                tb = tbp * TIP + ti
                tglob = tbase + tb * 128
                if res_fin is not None:
                    p.wait("sync", res_fin)
                xl = p.emit("sync", lambda e, d=xrs[:, 0:2048],
                            s=c.x_own[tmove * TT + tb * 128: tmove * TT + (tb + 1) * 128, :]:
                            e.dma_start(out=d, in_=s), dma=True)
                x0 = p.emit("sync", lambda e, d=scr3[:],
                            s=c.xn_tmD[tglob:tglob + 128, :]: e.dma_start(out=d, in_=s), dma=True)
                x1 = p.emit("sync", lambda e, d=xrs[:, 2048:4096],
                            s=c.xn_tmD[TOWN + tglob: TOWN + tglob + 128, :]:
                            e.dma_start(out=d, in_=s), dma=True)
                p.wait("vector", x1)
                p.wait("vector", xl)
                p.v(lambda e: e.tensor_sub(scr3[:], scr3[:], xrs[:, 2048:4096]))
                p.v(lambda e: e.scalar_tensor_tensor(scr3[:], scr3[:], c.m0c[:, 0:1],
                                                     xrs[:, 2048:4096], ALU.mult, ALU.add))
                for dm in range(4):
                    p.v(lambda e, d=acc[:, tb * 2048 + dm * 512: tb * 2048 + (dm + 1) * 512],
                        s=c.pb[ti * 4 + dm][:, :]: e.tensor_add(d, s, d))
                p.v(lambda e, d=acc[:, tb * 2048:(tb + 1) * 2048]:
                    e.tensor_add(d, d, xrs[:, 0:2048]))
                p.v(lambda e, d=acc[:, tb * 2048:(tb + 1) * 2048], s=scr3[:]:
                    e.tensor_add(d, d, s))
                fin = p.v(lambda e, d=acc[:, tb * 2048:(tb + 1) * 2048]:
                          e.tensor_add(d, d, sb2b[:]))
                res_fin = fin
                p.wait("gpsimd", fin)
                out_stores.append(p.emit("gpsimd", lambda e, tmove=tmove, tb=tb,
                                         s=acc[:, tb * 2048:(tb + 1) * 2048]:
                                         e.dma_start(out=c.out[tmove * TT + tb * 128:
                                                               tmove * TT + (tb + 1) * 128, :], in_=s),
                                         dma=True))
    emit_final_waits(c)
    sb3.close()


def pack_inputs(inp, core, L=4096, n_exp=NE):
    b = core // 2
    half = core % 2
    TOWN = L // 2
    x = np.ascontiguousarray(inp["L0_attn_hidden_states"][b])
    h0 = half * NHC
    heads = slice(h0, h0 + NHC)

    def T(a):
        return np.ascontiguousarray(np.transpose(a))

    qw = T(inp["L0_attn_q_w"])[:, h0 * HD:(h0 + NHC) * HD]
    kw = T(inp["L0_attn_k_w"])[:, h0 * HD:(h0 + NHC) * HD]
    vw = T(inp["L0_attn_v_w"])[:, h0 * HD:(h0 + NHC) * HD]
    ow = T(inp["L0_attn_o_w"])
    lrw = T(inp["L0_attn_ttt_lr_weight"][heads, 0, :])
    lrb = np.ascontiguousarray(inp["L0_attn_ttt_lr_bias"][heads])

    W1 = inp["L0_attn_W1"][heads]
    wst = np.zeros((128, 512), np.float32)
    for g in range(2):
        for pr in range(4):
            for a in range(2):
                hloc = 8 * g + 2 * pr + a
                wst[a * 64:(a + 1) * 64, (g * 4 + pr) * 64:(g * 4 + pr + 1) * 64] = W1[hloc]
    b1 = inp["L0_attn_b1"][heads, 0, :]
    b1t = np.zeros((128, 128), np.float32)
    gm = inp["L0_attn_ttt_norm_weight"][heads]
    bt = inp["L0_attn_ttt_norm_bias"][heads]
    gmt = np.zeros((128, 128), np.float32)
    btt = np.zeros((128, 128), np.float32)
    for g in range(2):
        for cc in range(8):
            hloc = 8 * g + cc
            rows = slice(16 * cc, 16 * cc + 16)
            b1t[rows, 64 * g:64 * g + 64] = b1[hloc]
            gmt[rows, 64 * g:64 * g + 64] = gm[hloc]
            btt[rows, 64 * g:64 * g + 64] = bt[hloc]
    lti = inp["L0_attn_learnable_token_idx"]
    lti_col = np.tile(lti.reshape(1, 16), (8, 1)).reshape(128, 1)
    lti15 = np.broadcast_to(lti[15:16], (128,)).reshape(128, 1).copy()
    invidx = np.tile((1.0 / np.arange(1, 17, dtype=np.float32)).reshape(1, 16), (8, 1)).reshape(128, 1)
    mk = np.zeros((128, 64), np.float32)
    tri = np.triu(np.ones((16, 16), np.float32))
    for pr in range(4):
        for g in range(2):
            mk[32 * pr:32 * pr + 16, 32 * g:32 * g + 16] = tri
            mk[32 * pr + 16:32 * pr + 32, 32 * g + 16:32 * g + 32] = tri
    onesbd = np.zeros((32, 32), np.float32)
    onesbd[0:16, 0:16] = 1.0
    onesbd[16:32, 16:32] = 1.0
    onesbd = np.tile(onesbd, (4, 1))
    bdm = np.zeros((128, 128), np.float32)
    for pr in range(4):
        bdm[32 * pr:32 * pr + 16, 0:64] = 1.0
        bdm[32 * pr + 16:32 * pr + 32, 64:128] = 1.0
    bdmask_rep = np.tile(bdm, (1, 16))
    import ml_dtypes
    bf = lambda a: a.astype(ml_dtypes.bfloat16)
    ident = np.eye(128, dtype=np.float32)
    m0 = np.full((128, 1), 1.0 - half, np.float32)

    wa = inp["L0_ffn_wa"]
    rg, rw1, rw2 = inp["L0_ffn_rg"], inp["L0_ffn_rw1"], inp["L0_ffn_rw2"]
    sg, sw1, sw2 = inp["L0_ffn_sg"], inp["L0_ffn_sw1"], inp["L0_ffn_sw2"]
    f_g = np.concatenate([rg.reshape(NRE * D, FH), sg.reshape(NSH * D, FH)], 0)
    f_w1 = np.concatenate([rw1.reshape(NRE * D, FH), sw1.reshape(NSH * D, FH)], 0)
    f_w2 = np.concatenate([rw2.reshape(NRE * FH, D), sw2.reshape(NSH * FH, D)], 0)
    rgb = np.concatenate([inp["L0_ffn_rgb"], inp["L0_ffn_sgb"]], 0)
    rb1 = np.concatenate([inp["L0_ffn_rb1"], inp["L0_ffn_sb1"]], 0)
    ne_all = rgb.shape[0]
    f_gb = np.ascontiguousarray(rgb.reshape(ne_all * 8, 128).T)
    f_b1 = np.ascontiguousarray(rb1.reshape(ne_all * 8, 128).T)

    im = {
        "x_b": x,
        "x_own": np.ascontiguousarray(x[half * TOWN:(half + 1) * TOWN]),
        "qw_t": np.ascontiguousarray(qw), "kw_t": np.ascontiguousarray(kw),
        "vw_t": np.ascontiguousarray(vw), "ow_t": np.ascontiguousarray(ow),
        "lrw_t": np.ascontiguousarray(lrw), "lrb": lrb,
        "w1_init": wst, "b1_init": b1t, "gam_t": gmt, "bet_t": btt,
        "lti_col": lti_col, "lti15_col": lti15, "invidx_col": invidx,
        "masketa": mk, "onesbd": bf(onesbd), "bdmask_rep": bf(bdmask_rep),
        "ident_bf": bf(ident), "ident_f": ident, "m0col": m0,
        "m0mask_b": bf(np.full((128, 1024), 1.0 - half, np.float32)),
        "m0mask_f": np.full((128, 2048), 1.0 - half, np.float32),
        "postw_row": inp["L0_attn_post_norm_weight"].reshape(1, D),
        "postb_row": inp["L0_attn_post_norm_bias"].reshape(1, D),
        "onesrow": np.ones((1, 128), np.float32),
        "wa_t": np.ascontiguousarray(wa),
        "f_g": f_g, "f_w1": f_w1, "f_w2": f_w2,
        "f_gb": f_gb, "f_b1": f_b1,
        "f_rb2": np.concatenate([inp["L0_ffn_rb2"],
                                 np.zeros((128 - NRE, D), np.float32)], 0),
        "f_sb2a": inp["L0_ffn_sb2"][0:1], "f_sb2b": inp["L0_ffn_sb2"][1:2],
    }
    return {k: np.ascontiguousarray(v) for k, v in im.items()}


def kernel(**inputs):
    L = inputs["L0_attn_hidden_states"].shape[1]
    nc = build_nc(L=L, phases=3)
    in_maps = [pack_inputs(inputs, core, L=L) for core in range(8)]
    res = run_bass_kernel_spmd(nc, in_maps, core_ids=list(range(8)))
    TOWN = L // 2
    out = np.zeros((4, L, D), np.float32)
    for core in range(8):
        b, half = core // 2, core % 2
        out[b, half * TOWN:(half + 1) * TOWN] = res.results[core]["out"]
    return out
